# version 24
# baseline (speedup 1.0000x reference)
import contextlib
import numpy as np
import ml_dtypes
import concourse.bass as bass
import concourse.mybir as mybir
from concourse.bass_utils import run_bass_kernel_spmd

F32 = mybir.dt.float32
BF16 = mybir.dt.bfloat16
I32 = mybir.dt.int32
AF = mybir.ActivationFunctionType
ALU = mybir.AluOpType
AX = mybir.AxisListType

D = 1024
H = 16
HD = 64
FF = 3072
NHC = 24
NDC = 8
EPS_LN = 1e-5
EPS_ATT = 1e-6
EPS_L2 = 1e-12
SCALE_ATT = float(64 ** -0.5)
DEQ = 1.0 / 127.0

ENGS = ("sync", "vector", "scalar", "tensor", "gpsimd")


class Edge:
    def __init__(self, es, nc, name):
        self.h = es.enter_context(nc.semaphore(name))
        self.n = 0


class Plan:
    def __init__(self):
        self.q = {e: [] for e in ENGS}
        self.lastw = {e: {} for e in ENGS}

    def op(self, eng, deps, inc, build, k=1):
        dd = []
        for ed, v in deps:
            if ed is None or v is None or v <= 0:
                continue
            key = id(ed)
            if self.lastw[eng].get(key, 0) >= v:
                continue
            self.lastw[eng][key] = v
            dd.append((ed.h, v))
        icap = None
        if inc is not None:
            inc.n += k
            icap = (inc.h, k)

        def thunk(eo, dd=tuple(dd), icap=icap, build=build):
            for h, v in dd:
                eo.wait_ge(h, v)
            ins = build(eo)
            if icap is not None:
                ins.then_inc(icap[0], icap[1])

        self.q[eng].append(thunk)
        return inc.n if inc is not None else None


def build_program(TCORE=4096, detect_races=False, debug_taps=False):
    NT = TCORE // 128
    TG = 512
    NTG = TCORE // TG
    NTT = TG // 128

    nc = bass.Bass("TRN2", target_bir_lowering=False, debug=False,
                   detect_race_conditions=detect_races)

    x_in = nc.dram_tensor("x", [TCORE, D], F32, kind="ExternalInput").ap()
    wqkv = nc.dram_tensor("wqkv", [D, 3 * D], F32, kind="ExternalInput").ap()
    wg_i = nc.dram_tensor("wg", [NHC, 128, D], I32, kind="ExternalInput").ap()
    wu_i = nc.dram_tensor("wu", [NHC, 128, D], I32, kind="ExternalInput").ap()
    wd_i = nc.dram_tensor("wd", [FF, D], I32, kind="ExternalInput").ap()
    ident_in = nc.dram_tensor("ident", [128, 128], BF16, kind="ExternalInput").ap()
    y_out = nc.dram_tensor("y", [TCORE, D], F32, kind="ExternalOutput").ap()

    wg_bf = nc.dram_tensor("wg_bf", [NHC, 128, 1024], BF16).ap()
    wu_bf = nc.dram_tensor("wu_bf", [NHC, 128, 1024], BF16).ap()
    qT_scr = nc.dram_tensor("qT_scr", [NT, 128, 1024], BF16).ap()
    xnT_scr = nc.dram_tensor("xnT_scr", [NT, 128, 1024], BF16).ap()
    cc_in = nc.dram_tensor("cc_in", [128, 520], F32)
    cc_out = nc.dram_tensor("cc_out", [128, 520], F32)
    dbg_qt = dbg_wd = dbg_h = dbg_kvbd = None
    if debug_taps:
        dbg_qt = nc.dram_tensor("dbg_qt", [NT, 128, 1024], BF16, kind="ExternalOutput").ap()
        dbg_wd = nc.dram_tensor("dbg_wd", [128, NHC * 1024], BF16, kind="ExternalOutput").ap()
        dbg_h = nc.dram_tensor("dbg_h", [128, NHC * 512], BF16, kind="ExternalOutput").ap()
        dbg_kvbd = nc.dram_tensor("dbg_kvbd", [128, 8 * 130], BF16, kind="ExternalOutput").ap()

    es = contextlib.ExitStack()
    with es:
        def sb(name, shape, dt):
            return es.enter_context(nc.sbuf_tensor(name, shape, dt))

        xb4 = sb("xb4", [128, 4 * D], F32)
        xb = [xb4.ap()[:, i * D:(i + 1) * D] for i in range(4)]
        qn = sb("qn", [128, D], F32)
        kn = sb("kn", [128, D], F32)
        eab = sb("eab", [128, D], BF16)
        ep2b = sb("ep2b", [128, D], F32)
        outb = [sb("outb0", [128, D], F32), sb("outb1", [128, D], F32)]
        sq_k = sb("sq_k", [128, D], BF16)
        xnb = [sb("xnb0", [128, D], BF16), sb("xnb1", [128, D], BF16)]
        qhat = [sb("qhat0", [128, D], BF16), sb("qhat1", [128, D], BF16)]
        khat = [sb("khat0", [128, D], BF16), sb("khat1", [128, D], BF16)]
        vtile = [sb("vt0", [128, H * 65], BF16), sb("vt1", [128, H * 65], BF16)]
        xnT_t = [sb("xnTt0", [128, D], BF16), sb("xnTt1", [128, D], BF16),
                 sb("xnTt2", [128, D], BF16), sb("xnTt3", [128, D], BF16)]
        qh_st = [sb("qhst0", [128, D], BF16), sb("qhst1", [128, D], BF16)]
        wcast = [sb("wc0", [128, 1024], BF16), sb("wc1", [128, 1024], BF16)]
        xnT_g = [sb("xnTg0", [128, NDC * TG], BF16), sb("xnTg1", [128, NDC * TG], BF16)]
        _xbt = xb4.ap()[:, 2 * D:4 * D].bitcast(BF16)
        wg_sb = [_xbt[:, 0:1024], _xbt[:, 1024:2048]]
        wu_sb = [_xbt[:, 2048:3072], _xbt[:, 3072:4096]]
        wd_sb = sb("wd_sb", [128, NHC * 1024], BF16)
        bigW = sb("bigW", [128, NDC * 3 * D], BF16)
        kv_bd = sb("kv_bd", [128, 8 * 130], BF16)
        ident = sb("ident_sb", [128, 128], BF16)
        wi = [sb("wi0", [128, 1024], I32), sb("wi1", [128, 1024], I32)]
        qT_t = [wi[0].ap().bitcast(BF16)[:, 0:1024], wi[1].ap().bitcast(BF16)[:, 0:1024]]
        cst = sb("cst", [128, 3], F32)
        ssum = sb("ssum", [128, 4], F32)
        ssq = sb("ssq", [128, 4], F32)
        mb = sb("mb", [128, 4], F32)
        mm2 = sb("mm2", [128, 4], F32)
        varb = sb("varb", [128, 4], F32)
        stdv = sb("stdv", [128, 4], F32)
        rstd = sb("rstd", [128, 4], F32)
        negmr = sb("negmr", [128, 4], F32)
        ssq_q = sb("ssq_q", [128, H], F32)
        ssq_k = sb("ssq_k", [128, H], F32)
        nrm2_q = sb("nrm2_q", [128, H], F32)
        nrm2_k = sb("nrm2_k", [128, H], F32)
        rst_q = sb("rst_q", [128, H], F32)
        rst_k = sb("rst_k", [128, H], F32)
        rden = [sb("rden0", [128, H], F32), sb("rden1", [128, H], F32)]

        sqdump = eab.ap()[:, 0:D]
        cc_sb = outb[1].ap()[:, 0:520]
        sg = [kn.ap()[:, 0:512], kn.ap()[:, 512:1024]]
        ep_scr1 = qn.ap()[:, 0:D]
        ep_scr2 = ep2b.ap()[:]
        P_rA = es.enter_context(nc.psum_tensor("P_rA", [128, 1024], F32))
        P_rB = es.enter_context(nc.psum_tensor("P_rB", [128, 1024], F32))
        P_kv = es.enter_context(nc.psum_tensor("P_kv", [128, 1024], F32))
        P_m1 = es.enter_context(nc.psum_tensor("P_m1", [128, 512], F32))
        P_m2 = es.enter_context(nc.psum_tensor("P_m2", [128, 512], F32))
        P_m1b = P_m1.ap().bitcast(BF16)
        P_m2b = P_m2.ap().bitcast(BF16)

        E_gd = Edge(es, nc, "E_gd")
        E_t = Edge(es, nc, "E_t")
        E_v = Edge(es, nc, "E_v")
        E_sc = Edge(es, nc, "E_sc")
        E_cc = Edge(es, nc, "E_cc")
        E_x = [Edge(es, nc, f"E_x{i}") for i in range(4)]
        E_cvl = [Edge(es, nc, f"E_cvl{i}") for i in range(8)]
        E_cvst = [Edge(es, nc, f"E_cvst{i}") for i in range(8)]
        E_xnt = [Edge(es, nc, f"E_xnt{i}") for i in range(4)]
        E_xnl = [Edge(es, nc, f"E_xnl{i}") for i in range(4)]
        E_qt = [Edge(es, nc, f"E_qt{i}") for i in range(2)]
        E_xg = [Edge(es, nc, f"E_xg{i}") for i in range(2)]
        E_wg = [Edge(es, nc, f"E_wg{i}") for i in range(2)]
        E_wu = [Edge(es, nc, f"E_wu{i}") for i in range(2)]
        E_x3 = [Edge(es, nc, f"E_x3{i}") for i in range(2)]
        E_out = [Edge(es, nc, f"E_out{i}") for i in range(2)]
        E_wq = [Edge(es, nc, f"E_wq{i}") for i in range(2)]
        E_qtl = [Edge(es, nc, f"E_qtl{i}") for i in range(2)]
        E_ccin = Edge(es, nc, "E_ccin")
        E_ccout = Edge(es, nc, "E_ccout")
        E_misc = Edge(es, nc, "E_misc")

        P = Plan()
        op = P.op

        s_x = {}; sc_sq = {}; v_var = {}; sc_sqrt = {}; v_stats = {}; sc_xn = {}
        t_trx = {}; v_evacX = {}; s_xnt = {}
        t_q = {}; sc_qsq = {}; v_gq = {}; sc_nq = {}; v_qmul = {}; sc_expq = {}; sc_qhat = {}
        t_k = {}; sc_ksq = {}; v_gk = {}; sc_nk = {}; v_kmul = {}; sc_expk = {}; sc_khat = {}
        t_v = {}; v_vev = {}; t_kv = {}; t_trq = {}; v_evacQ = {}; s_qt = {}
        gd_wg = {}; sc_cwg = {}; s_swg = {}
        gd_wu = {}; v_cwu = {}; s_swu = {}
        gd_wd = {}; sc_cwd = {}
        t_gate = {}; sc_silu = {}; t_up = {}; v_hmul = {}
        s_w = {}; s_xg = {}; s_x3 = {}; sd_qt = {}
        t_down = {}; t_attn = {}
        v_ep0 = {}; v_ep1 = {}; v_ep2 = {}; v_ep3 = {}; s_out = {}

        op("sync", [], E_misc, lambda e: e.dma_start(out=ident[:], in_=ident_in), k=16)
        s_ident = (E_misc, E_misc.n)
        op("vector", [], E_v, lambda e: e.memset(vtile[0][:], 1.0))
        op("vector", [], E_v, lambda e: e.memset(vtile[1][:], 1.0))
        op("vector", [], E_v, lambda e: e.memset(kv_bd[:], 0.0))
        op("vector", [], E_v, lambda e: e.memset(cst[:, 0:1], EPS_LN))
        op("vector", [], E_v, lambda e: e.memset(cst[:, 1:2], 1.0))
        op("vector", [], E_v, lambda e: e.memset(cst[:, 2:3], EPS_L2 * EPS_L2))
        v_init = E_v.n

        wq_cast = {}
        wq_sc_last = [0]
        wq_v_last = [0]

        wq_ld = {}

        def wq_load(kk):
            if kk in wq_ld:
                return
            if kk >= 2 and (kk - 2) not in wq_cast:
                return
            dc, s = divmod(kk, 3)
            par = kk % 2
            deps = []
            if kk >= 2:
                pe, pv = wq_cast[kk - 2]
                deps = [((E_sc if pe == "scalar" else E_v), pv)]
            op("sync", deps, E_wq[par],
               lambda e, dc=dc, s=s, par=par: e.dma_start(
                   out=outb[par][:], in_=wqkv[dc * 128:(dc + 1) * 128, s * 1024:(s + 1) * 1024]), k=16)
            wq_ld[kk] = (E_wq[par], E_wq[par].n)

        def wq_cast_task(kk):
            wq_load(kk)
            par = kk % 2
            dc, s = divmod(kk, 3)
            eng = "scalar" if (kk % 2 == 0) else "vector"
            ed = E_sc if eng == "scalar" else E_v
            col = dc * 3 * D + s * 1024
            if eng == "scalar":
                op(eng, [wq_ld[kk]], ed,
                   lambda e, par=par, col=col: e.activation(
                       bigW[:, col:col + 1024], outb[par][:], AF.Copy))
                wq_sc_last[0] = ed.n
            else:
                op(eng, [wq_ld[kk]], ed,
                   lambda e, par=par, col=col: e.tensor_copy(bigW[:, col:col + 1024], outb[par][:]))
                wq_v_last[0] = ed.n
            wq_cast[kk] = (eng, ed.n)

        def conv_sched(t):
            return range(t * NHC // NT, (t + 1) * NHC // NT)

        sc_qcopy = {}
        v_kcopy = {}
        v_xn = {}
        sc_evacQ = {}
        sc_qsoft = {}
        v_qhat = {}
        sc_khat = {}
        v_gq = {}
        v_gk = {}
        NP = NT // 2

        def front_x(t):
            op("sync", [(E_v, v_xn.get(t - 4)), (E_sc, sc_sq.get(t - 4))], E_x[t % 4],
               lambda e, t=t: e.dma_start(out=xb[t % 4], in_=x_in[t * 128:(t + 1) * 128, :]), k=16)
            s_x[t] = (E_x[t % 4], E_x[t % 4].n)

        def round_mm(t, base, Pr, deps):
            tpar = t % 4
            for dc in range(NDC):
                for hf in range(2):
                    ed = E_t if (dc == NDC - 1 and hf == 1) else None
                    op("tensor", deps if (dc == 0 and hf == 0) else [], ed,
                       lambda e, dc=dc, hf=hf, tpar=tpar, Pr=Pr, base=base: e.matmul(
                           Pr.ap()[:, hf * 512:(hf + 1) * 512],
                           xnT_t[tpar][:, dc * 128:(dc + 1) * 128],
                           bigW[:, dc * 3 * D + base + hf * 512: dc * 3 * D + base + (hf + 1) * 512],
                           start=(dc == 0), stop=(dc == NDC - 1)))
            return E_t.n

        def kv_col(p):
            return (p // 4) * 512 + (p % 4) * 65

        def backT_kv(t):
            deps = [(E_sc, sc_khat[t]), (E_v, v_vev[t])]
            first = True
            for p in range(8):
                for eh in range(2):
                    h = 2 * p + eh
                    st = (t == 0) and (p % 4 == 0)
                    sp = (t == NT - 1) and (p % 4 == 3)
                    last = (p == 7) and (eh == 1)
                    op("tensor", deps if first else [], E_t if last else None,
                       lambda e, p=p, eh=eh, h=h, st=st, sp=sp, t=t: e.matmul(
                           P_kv.ap()[eh * 64:(eh + 1) * 64, kv_col(p):kv_col(p) + 65],
                           khat[t % 2][:, h * 64:(h + 1) * 64],
                           vtile[t % 2][:, h * 65:(h + 1) * 65],
                           start=st, stop=sp, skip_group_check=True))
                    first = False
            t_kv[t] = E_t.n

        cv_in = [wi[0].ap(), wi[1].ap(),
                 qn.ap().bitcast(I32), kn.ap().bitcast(I32),
                 xnT_g[0].ap().bitcast(I32)[:, 0:1024],
                 xnT_g[0].ap().bitcast(I32)[:, 1024:2048],
                 xnT_g[1].ap().bitcast(I32)[:, 0:1024],
                 xnT_g[1].ap().bitcast(I32)[:, 1024:2048]]
        cv_out = [wcast[0].ap()[:], wcast[1].ap()[:], qhat[0].ap()[:], khat[0].ap()[:],
                  qh_st[0].ap()[:], qh_st[1].ap()[:], sq_k.ap()[:],
                  ep2b.ap().bitcast(BF16)[:, 0:1024]]
        NSTG = 8
        cv_ld = {}
        cv_cast = {}
        cv_st = {}
        cv_tasks = []
        for hc in range(NHC):
            cv_tasks += [(0, hc), (1, hc), (2, hc)]

        def conv_store(j):
            kind, hc = cv_tasks[j]
            if kind == 2:
                return
            b = j % NSTG
            pe, pv = cv_cast[j]
            dstd = wg_bf[hc] if kind == 0 else wu_bf[hc]
            op("scalar", [((E_sc if pe == "scalar" else E_v), pv)], E_cvst[b],
               lambda e, dstd=dstd, b=b: e.dma_start(out=dstd, in_=cv_out[b]), k=16)
            cv_st[j] = (E_cvst[b], E_cvst[b].n)

        def conv_task(k):
            kind, hc = cv_tasks[k]
            b = k % NSTG
            deps = []
            if k >= NSTG:
                pe, pv = cv_cast[k - NSTG]
                deps.append(((E_sc if pe == "scalar" else E_v), pv))
                if cv_st[k - NSTG] is not None:
                    deps.append(cv_st[k - NSTG])
            if kind == 0:
                op("gpsimd", deps, E_cvl[b],
                   lambda e, hc=hc, b=b: e.dma_start(out=cv_in[b], in_=wg_i[hc]), k=16)
            elif kind == 1:
                op("gpsimd", deps, E_cvl[b],
                   lambda e, hc=hc, b=b: e.dma_start(out=cv_in[b], in_=wu_i[hc]), k=16)
            else:
                op("gpsimd", deps, E_cvl[b],
                   lambda e, hc=hc, b=b: e.dma_start(
                       out=cv_in[b], in_=wd_i[hc * 128:(hc + 1) * 128, :]), k=16)
            cv_ld[k] = (E_cvl[b], E_cvl[b].n)
            eng = "scalar" if (k % 2 == 0) else "vector"
            ed = E_sc if eng == "scalar" else E_v
            if kind == 2:
                dst = wd_sb.ap()[:, hc * 1024:(hc + 1) * 1024]
            else:
                dst = cv_out[b]
            if eng == "scalar":
                op(eng, [cv_ld[k]], ed,
                   lambda e, b=b, dst=dst: e.activation(dst, cv_in[b], AF.Copy, scale=DEQ))
            else:
                op(eng, [cv_ld[k]], ed,
                   lambda e, b=b, dst=dst: e.tensor_scalar_mul(dst, cv_in[b], DEQ))
            cv_cast[k] = (eng, ed.n)
            if kind == 2:
                cv_st[k] = None
                sc_cwd[hc] = ed.n if eng == "scalar" else None
                g_conv[hc] = (E_sc if eng == "scalar" else E_v, ed.n)
            else:
                conv_store(k)

        def conv(tp):
            k0 = tp * len(cv_tasks) // NP
            k1 = (tp + 1) * len(cv_tasks) // NP
            for k in range(k0, k1):
                conv_task(k)

        g_conv = {}

        qb = [qn.ap().bitcast(BF16)[:, 0:1024], qn.ap().bitcast(BF16)[:, 1024:2048]]
        kb = [kn.ap().bitcast(BF16)[:, 0:1024], kn.ap().bitcast(BF16)[:, 1024:2048]]

        def ladder(t, which):
            par = t % 2
            Pr = ((P_rA if t % 2 == 0 else P_rB) if which == 0
                  else (P_rB if t % 2 == 0 else P_rA))
            if which == 0:
                qc_deps = [(E_t, t_q[t]), (E_v, v_qhat.get(t - 2))]
                if t == 0:
                    qc_deps.append(cv_stage_guard.get(2))
                op("scalar", [d for d in qc_deps if d], E_sc,
                   lambda e, Pr=Pr, par=par: e.copy(qb[par], Pr.ap()[:]))
                sc_qcopy[t] = E_sc.n
                op("scalar", [(E_sc, E_sc.n), (E_v, v_gq.get(t - 1))], E_sc,
                   lambda e, par=par: e.activation(eab[:], qb[par], AF.Square))
                op("vector", [(E_sc, E_sc.n)], E_v,
                   lambda e: e.reduce_sum(
                       ssq_q[:], eab[:].rearrange("p (h d) -> p h d", h=H), axis=AX.X))
                v_gq[t] = E_v.n
                op("scalar", [(E_v, E_v.n)], E_sc,
                   lambda e: e.activation(nrm2_q[:], ssq_q[:], AF.Ln, bias=cst[:, 2:3]))
                op("scalar", [(E_sc, E_sc.n)], E_sc,
                   lambda e: e.activation(rst_q[:], nrm2_q[:], AF.Exp, scale=-0.5))
                qm_deps = [(E_sc, E_sc.n), (E_t, t_trq.get(t - 2))]
                if t == 0:
                    qm_deps.append(cv_out_guard.get(2))
                op("vector", [d for d in qm_deps if d], E_v,
                   lambda e, par=par: e.tensor_tensor(
                       qhat[par][:].rearrange("p (h d) -> p h d", h=H),
                       qb[par].rearrange("p (h d) -> p h d", h=H),
                       rst_q[:].unsqueeze(2).broadcast_to([128, H, HD]),
                       ALU.mult))
                v_qhat[t] = E_v.n
            else:
                kc_deps = [(E_t, t_k[t]), (E_sc, sc_khat.get(t - 2))]
                if t == 0:
                    kc_deps.append(cv_stage_guard.get(3))
                op("vector", [d for d in kc_deps if d], E_v,
                   lambda e, Pr=Pr, par=par: e.tensor_copy(kb[par], Pr.ap()[:]))
                v_kcopy[t] = E_v.n
                ks_deps = [(E_v, E_v.n), (E_v, v_gk.get(t - 1))]
                if t == 0:
                    ks_deps.append(cv_out_guard.get(6))
                op("scalar", [d for d in ks_deps if d], E_sc,
                   lambda e, par=par: e.activation(sq_k[:], kb[par], AF.Square))
                op("vector", [(E_sc, E_sc.n)], E_v,
                   lambda e: e.reduce_sum(
                       ssq_k[:], sq_k[:].rearrange("p (h d) -> p h d", h=H), axis=AX.X))
                v_gk[t] = E_v.n
                op("scalar", [(E_v, E_v.n)], E_sc,
                   lambda e: e.activation(nrm2_k[:], ssq_k[:], AF.Ln, bias=cst[:, 2:3]))
                op("scalar", [(E_sc, E_sc.n)], E_sc,
                   lambda e: e.activation(rst_k[:], nrm2_k[:], AF.Exp, scale=-0.5))
                op("vector", [(E_sc, E_sc.n)], E_v,
                   lambda e, par=par: e.tensor_tensor(
                       eab[:].rearrange("p (h d) -> p h d", h=H),
                       kb[par].rearrange("p (h d) -> p h d", h=H),
                       rst_k[:].unsqueeze(2).broadcast_to([128, H, HD]),
                       ALU.mult))
                v_kmul[t] = E_v.n
                op("scalar", [(E_v, E_v.n)], E_sc,
                   lambda e: e.activation(sq_k[:], eab[:], AF.Exp))
                lk_deps = [(E_sc, E_sc.n), (E_t, t_kv.get(t - 2))]
                if t == 0:
                    lk_deps.append(cv_out_guard.get(3))
                op("scalar", [d for d in lk_deps if d], E_sc,
                   lambda e, par=par: e.activation(khat[par][:], sq_k[:], AF.Ln, bias=cst[:, 1:2]))
                sc_khat[t] = E_sc.n

        def back_tile(t):
            t_q[t] = round_mm(t, 0, P_rA if t % 2 == 0 else P_rB,
                              [s_xnl[t], (E_v, v_kcopy.get(t - 1)),
                               (E_sc, sc_wq), (E_v, v_wq)])
            ladder(t, 0)
            t_k[t] = round_mm(t, D, P_rB if t % 2 == 0 else P_rA,
                              [(E_v, v_vev.get(t - 1))])
            ladder(t, 1)
            t_v[t] = round_mm(t, 2 * D, P_rA if t % 2 == 0 else P_rB,
                              [(E_sc, sc_qcopy[t])])
            op("vector", [(E_t, t_v[t]), (E_t, t_kv.get(t - 2)), (E_v, v_init)], E_v,
               lambda e, t=t: e.tensor_copy(
                   vtile[t % 2][:].rearrange("p (h c) -> p h c", h=H)[:, :, 0:64],
                   (P_rA if t % 2 == 0 else P_rB).ap().rearrange("p (h d) -> p h d", h=H)))
            v_vev[t] = E_v.n

        def trq_block(t):
            Pm = P_m1b if t % 2 == 0 else P_m2b
            depsq = [(E_v, v_qhat[t]), (E_sc, sc_evacQ.get(t - 2))]
            if t < 2:
                depsq.append((E_v, v_evacX.get(NT - 1)))
            for dc in range(NDC):
                op("tensor", depsq if dc == 0 else [],
                   E_t if dc == NDC - 1 else None,
                   lambda e, dc=dc, t=t, Pm=Pm: e.transpose(
                       Pm[:, dc * 128:(dc + 1) * 128],
                       qhat[t % 2][:, dc * 128:(dc + 1) * 128], ident[:]))
            t_trq[t] = E_t.n
            eq_deps = [(E_t, E_t.n)] + ([s_qt[t - 2]] if t >= 2 else [])
            if t < 2:
                eq_deps.append(cv_out_guard.get(4 + t))
            op("scalar", [d for d in eq_deps if d], E_sc,
               lambda e, t=t, Pm=Pm: e.copy(qh_st[t % 2][:], Pm[:, 0:1024]))
            sc_evacQ[t] = E_sc.n
            op("sync", [(E_sc, E_sc.n)], E_qt[t % 2],
               lambda e, t=t: e.dma_start(out=qT_scr[t], in_=qh_st[t % 2][:]), k=16)
            s_qt[t] = (E_qt[t % 2], E_qt[t % 2].n)

        def front_tile(t, i, fp):
            sl = 2 * (fp % 2)
            op("vector", [(E_sc, sc_sqrt[fp]), (E_v, v_stats[fp]),
                          (E_t, t_trx.get(t - 2))], E_v,
               lambda e, i=i, t=t, sl=sl: e.tensor_scalar(
                   xnb[t % 2][:], xb[t % 4], negmr[:, sl + i:sl + i + 1], rstd[:, sl + i:sl + i + 1],
                   ALU.add, ALU.mult))
            v_xn[t] = E_v.n
            Pm = P_m1b if t % 2 == 0 else P_m2b
            depsx = [(E_v, v_xn[t]), (E_v, v_evacX.get(t - 2)), s_ident]
            for dc in range(NDC):
                op("tensor", depsx if dc == 0 else [],
                   E_t if dc == NDC - 1 else None,
                   lambda e, dc=dc, t=t, Pm=Pm: e.transpose(
                       Pm[:, dc * 128:(dc + 1) * 128],
                       xnb[t % 2][:, dc * 128:(dc + 1) * 128], ident[:]))
            t_trx[t] = E_t.n
            op("vector", [(E_t, E_t.n)] + ([s_xnt[t - 4]] if t >= 4 else []), E_v,
               lambda e, t=t, Pm=Pm: e.tensor_copy(xnT_t[t % 4][:], Pm[:, 0:1024]))
            v_evacX[t] = E_v.n
            op("sync", [(E_v, v_evacX[t])], E_xnt[t % 4],
               lambda e, t=t: e.dma_start(out=xnT_scr[t], in_=xnT_t[t % 4][:]), k=16)
            s_xnt[t] = (E_xnt[t % 4], E_xnt[t % 4].n)

        front_x(0)
        front_x(1)
        wq_n = NDC * 3
        wq_np = min(NP, 12)

        def wq_range(fp):
            if fp >= wq_np:
                return range(0)
            return range(fp * wq_n // wq_np, (fp + 1) * wq_n // wq_np)

        for fp in range(NP):
            if fp + 1 < NP:
                front_x(2 * fp + 2)
                front_x(2 * fp + 3)
            for kk in wq_range(fp):
                wq_load(kk)
            if True:
                a, b = 2 * fp, 2 * fp + 1
                sl = 2 * (fp % 2)
                op("scalar", [s_x[a], (E_v, v_stats.get(fp - 2)), (E_v, v_var.get(fp - 2))], E_sc,
                   lambda e, a=a, sl=sl: e.activation(sqdump, xb[a % 4], AF.Square, accum_out=ssq[:, sl:sl + 1]))
                op("scalar", [s_x[b]], E_sc,
                   lambda e, b=b, sl=sl: e.activation(sqdump, xb[b % 4], AF.Square, accum_out=ssq[:, sl + 1:sl + 2]))
                sc_sq[a] = sc_sq[b] = E_sc.n
                op("vector", [s_x[a], s_x[b], (E_v, v_stats.get(fp - 2))], E_v,
                   lambda e, a=a, sl=sl: e.reduce_sum(
                       ssum[:, sl:sl + 2], xb4.ap()[:, (a % 4) * D:(a % 4 + 2) * D]
                       .rearrange("p (i d) -> p i d", i=2), axis=AX.X))
                op("vector", [(E_v, E_v.n)], E_v,
                   lambda e, sl=sl: e.tensor_scalar_mul(mb[:, sl:sl + 2], ssum[:, sl:sl + 2], 1.0 / D))
                op("vector", [(E_v, E_v.n)], E_v,
                   lambda e, sl=sl: e.tensor_mul(mm2[:, sl:sl + 2], mb[:, sl:sl + 2], mb[:, sl:sl + 2]))
                op("vector", [(E_v, E_v.n), (E_sc, sc_sq[b])], E_v,
                   lambda e, sl=sl: e.tensor_scalar(varb[:, sl:sl + 2], ssq[:, sl:sl + 2], 1.0 / D, EPS_LN, ALU.mult, ALU.add))
                op("vector", [(E_v, E_v.n)], E_v,
                   lambda e, sl=sl: e.tensor_sub(varb[:, sl:sl + 2], varb[:, sl:sl + 2], mm2[:, sl:sl + 2]))
                v_var[fp] = E_v.n
                op("scalar", [(E_v, E_v.n)], E_sc,
                   lambda e, sl=sl: e.activation(stdv[:, sl:sl + 2], varb[:, sl:sl + 2], AF.Ln))
                op("scalar", [(E_sc, E_sc.n)], E_sc,
                   lambda e, sl=sl: e.activation(rstd[:, sl:sl + 2], stdv[:, sl:sl + 2], AF.Exp, scale=-0.5))
                sc_sqrt[fp] = E_sc.n
                op("vector", [(E_v, v_var[fp])], E_v,
                   lambda e, sl=sl: e.tensor_scalar_mul(negmr[:, sl:sl + 2], mb[:, sl:sl + 2], -1.0))
                v_stats[fp] = E_v.n
                front_tile(a, 0, fp)
                front_tile(b, 1, fp)
            for kk in wq_range(fp):
                wq_cast_task(kk)
            conv(fp)
        sc_wq = wq_sc_last[0]
        v_wq = wq_v_last[0]

        cv_stage_guard = {}
        cv_out_guard = {}
        for k in range(len(cv_tasks)):
            b = k % NSTG
            pe, pv = cv_cast[k]
            cv_stage_guard[b] = ((E_sc if pe == "scalar" else E_v), pv)
            if cv_st.get(k) is not None:
                cv_out_guard[b] = cv_st[k]
        wd_guards = []
        _smax = _vmax = 0
        for k in range(len(cv_tasks)):
            if cv_tasks[k][0] == 2:
                pe, pv = cv_cast[k]
                if pe == "scalar":
                    _smax = max(_smax, pv)
                else:
                    _vmax = max(_vmax, pv)
        if _smax:
            wd_guards.append((E_sc, _smax))
        if _vmax:
            wd_guards.append((E_v, _vmax))

        s_xnl = {}

        def xnl(t):
            op("sync", [(E_t, t_v.get(t - 4)), (E_xnt[t % 4], E_xnt[t % 4].n)], E_xnl[t % 4],
               lambda e, t=t: e.dma_start(out=xnT_t[t % 4][:], in_=xnT_scr[t]), k=16)
            s_xnl[t] = (E_xnl[t % 4], E_xnl[t % 4].n)

        for ip in range(NP + 2):
            lp = ip if ip < NP else None
            bp = ip - 1 if 1 <= ip <= NP else None
            kp = ip - 2 if ip >= 2 else None
            if lp is not None:
                xnl(2 * lp)
                xnl(2 * lp + 1)
            if kp is not None:
                trq_block(2 * kp)
                trq_block(2 * kp + 1)
                backT_kv(2 * kp)
                backT_kv(2 * kp + 1)
            if bp is not None:
                back_tile(2 * bp)
                back_tile(2 * bp + 1)
        op("vector", [(E_t, t_kv[NT - 1])], E_v,
           lambda e: e.tensor_copy(
               cc_sb.rearrange("p (b q c) -> p b q c", b=2, q=4),
               P_kv.ap().rearrange("p (b f) -> p b f", b=2)[:, :, 0:260]
               .rearrange("p b (q c) -> p b q c", q=4)))
        v_ccevac = E_v.n
        op("scalar", [(E_v, E_v.n)], E_ccin,
           lambda e: e.dma_start(out=cc_in.ap(), in_=cc_sb), k=16)
        op("gpsimd", [(E_ccin, 16)], E_cc,
           lambda e: e.collective_compute(
               "AllReduce", ALU.add,
               replica_groups=[[0, 1], [2, 3], [4, 5], [6, 7]],
               ins=[cc_in.ap().opt()], outs=[cc_out.ap().opt()]))

        ccv = cc_sb.rearrange("p (b q c) -> p b q c", b=2, q=4)

        def build_kvbd():
            deps = [(E_ccout, 16), (E_v, v_init)]
            first = True
            for p in range(8):
                for eh in range(2):
                    op("vector", deps if first else [], E_v,
                       lambda e, p=p, eh=eh: e.tensor_scalar_mul(
                           kv_bd[eh * 64:(eh + 1) * 64, p * 130 + eh * 64: p * 130 + eh * 64 + 64],
                           ccv[eh * 64:(eh + 1) * 64, p // 4, p % 4, 0:64], SCALE_ATT))
                    first = False
                    op("vector", [], E_v,
                       lambda e, p=p, eh=eh: e.tensor_scalar_add(
                           kv_bd[eh * 64:(eh + 1) * 64, p * 130 + 128 + eh: p * 130 + 129 + eh],
                           ccv[eh * 64:(eh + 1) * 64, p // 4, p % 4, 64:65], EPS_ATT))
            return E_v.n

        v_kvbd = None
        wg_rd = [None, None]
        wu_rd = [None, None]
        xg_rd = [None, None]
        sg_rd = [None, None]
        silu_last = None
        hmul_last = None
        down_last = None
        for g in range(NTG):
            gpar = g % 2
            depsg = [(ex, ex.n) for ex in E_xnt] + [(E_t, xg_rd[gpar])]
            if g < 2:
                depsg += [d for d in (cv_stage_guard.get(4 + 2 * g),
                                      cv_stage_guard.get(5 + 2 * g)) if d]
            first = True
            for tb in range(NTT):
                op("sync", depsg if first else [], E_xg[gpar],
                   lambda e, tb=tb, g=g, gpar=gpar: e.dma_start(
                       out=xnT_g[gpar][:].rearrange("p (dc tb c) -> p dc tb c", dc=NDC, tb=NTT)
                       [:, :, tb, :],
                       in_=xnT_scr[g * NTT + tb].rearrange("p (dc c) -> p dc c", dc=NDC)), k=16)
                first = False
            s_xg[g] = (E_xg[gpar], E_xg[gpar].n)
            for hc in range(NHC):
                hpar = hc % 2
                if g == 0 and hc == 2:
                    op("scalar", [(E_cc, 1)], E_ccout,
                       lambda e: e.dma_start(out=cc_sb, in_=cc_out.ap()), k=16)
                wgl_deps = [(E_t, wg_rd[hpar])]
                wul_deps = [(E_t, wu_rd[hpar])]
                if g == 0 and hc < 2:
                    cvst_tot = [(ec, ec.n) for ec in E_cvst]
                    wgl_deps += cvst_tot + [(E_v, v_xn.get(NT - 1)), (E_sc, sc_sq.get(NT - 1))]
                    wul_deps += cvst_tot
                op("sync", wgl_deps, E_wg[hpar],
                   lambda e, hc=hc, hpar=hpar: e.dma_start(out=wg_sb[hpar], in_=wg_bf[hc]), k=16)
                op("sync", wul_deps, E_wu[hpar],
                   lambda e, hc=hc, hpar=hpar: e.dma_start(out=wu_sb[hpar], in_=wu_bf[hc]), k=16)
                s_w[(g, hc)] = [(E_wg[hpar], E_wg[hpar].n), (E_wu[hpar], E_wu[hpar].n)]
                depsgate = s_w[(g, hc)] + [s_xg[g],
                            (E_sc, silu_last),
                            (E_sc, sc_evacQ.get(NT - 2))] + wd_guards
                for dc in range(NDC):
                    op("tensor", depsgate if dc == 0 else [],
                       E_t if dc == NDC - 1 else None,
                       lambda e, dc=dc, hpar=hpar, gpar=gpar: e.matmul(
                           P_m1.ap()[:],
                           wg_sb[hpar][:, dc * 128:(dc + 1) * 128],
                           xnT_g[gpar][:, dc * TG:(dc + 1) * TG],
                           start=(dc == 0), stop=(dc == NDC - 1)))
                t_gate[(g, hc)] = E_t.n
                wg_rd[hpar] = E_t.n
                sdeps = [(E_t, E_t.n), (E_v, sg_rd[hpar])]
                if g == 0 and hc < 2:
                    sdeps.append((E_v, v_kmul.get(NT - 1)))
                op("scalar", sdeps, E_sc,
                   lambda e, hpar=hpar: e.activation(sg[hpar], P_m1.ap()[:], AF.Silu))
                sc_silu[(g, hc)] = E_sc.n
                silu_last = E_sc.n
                depsup = [(E_v, hmul_last)]
                if g == 0 and hc == 0:
                    depsup.append((E_sc, sc_evacQ.get(NT - 1)))
                for dc in range(NDC):
                    op("tensor", depsup if dc == 0 else [],
                       E_t if dc == NDC - 1 else None,
                       lambda e, dc=dc, hpar=hpar, gpar=gpar: e.matmul(
                           P_m2.ap()[:],
                           wu_sb[hpar][:, dc * 128:(dc + 1) * 128],
                           xnT_g[gpar][:, dc * TG:(dc + 1) * TG],
                           start=(dc == 0), stop=(dc == NDC - 1)))
                t_up[(g, hc)] = E_t.n
                wu_rd[hpar] = E_t.n
                xg_rd[gpar] = E_t.n
                depsh = [(E_t, E_t.n), (E_sc, sc_silu[(g, hc)]), (E_t, down_last)]
                if g == 0 and hc == 0:
                    depsh.append((E_t, t_v.get(NT - 1)))
                op("vector", depsh, E_v,
                   lambda e, hc=hc, hpar=hpar: e.tensor_tensor(
                       bigW[:, hc * TG:(hc + 1) * TG], P_m2.ap()[:], sg[hpar], ALU.mult))
                v_hmul[(g, hc)] = E_v.n
                sg_rd[hpar] = E_v.n
                hmul_last = E_v.n
            if g == 0:
                v_kvbd = build_kvbd()
            for tt in range(NTT):
                ttg = g * NTT + tt
                tpar = tt % 2
                P_dn = P_kv if tpar == 0 else P_rB
                P_de = P_rB if tpar == 0 else P_kv
                xdeps = [(E_v, v_ep2.get(ttg - 2))]
                if ttg < 2:
                    xdeps.append((E_v, v_xn.get(NT - 1)))
                    xdeps.append((E_sc, sc_sq.get(NT - 1)))
                op("sync", xdeps, E_x3[tpar],
                   lambda e, ttg=ttg, tpar=tpar: e.dma_start(
                       out=xb[tpar], in_=x_in[ttg * 128:(ttg + 1) * 128, :]), k=16)
                s_x3[ttg] = (E_x3[tpar], E_x3[tpar].n)
                if ttg >= 1:
                    op("sync", [(E_v, v_ep3[ttg - 1])], E_out[(ttg - 1) % 2],
                       lambda e, tg=ttg - 1: e.dma_start(
                           out=y_out[tg * 128:(tg + 1) * 128, :], in_=outb[tg % 2][:]), k=16)
                    s_out[ttg - 1] = (E_out[(ttg - 1) % 2], E_out[(ttg - 1) % 2].n)
                qtdeps = [(E_t, t_attn.get(ttg - 2)), (E_qt[ttg % 2], E_qt[ttg % 2].n)]
                if ttg < 2:
                    qtdeps += [cv_stage_guard.get(0), cv_stage_guard.get(1)]
                qtdeps = [d for d in qtdeps if d]
                op("scalar", qtdeps, E_qtl[tpar],
                   lambda e, ttg=ttg, tpar=tpar: e.dma_start(out=qT_t[tpar], in_=qT_scr[ttg]), k=16)
                sd_qt[ttg] = (E_qtl[tpar], E_qtl[tpar].n)
                sfdeps = [sd_qt[ttg]]
                if ttg == 0:
                    sfdeps.append((E_v, v_kmul.get(NT // 2 - 1)))
                op("scalar", sfdeps, E_sc,
                   lambda e, tpar=tpar: e.activation(eab[:, 0:D], qT_t[tpar], AF.Exp))
                op("scalar", [(E_sc, E_sc.n)], E_sc,
                   lambda e, tpar=tpar: e.activation(qT_t[tpar], eab[:, 0:D], AF.Ln, bias=cst[:, 1:2]))
                sc_qsoft[ttg] = E_sc.n
                depsd = [(E_v, v_hmul[(g, NHC - 1)]), (E_v, v_ep3.get(ttg - 2)),
                         (E_v, v_ep0.get(ttg - 1))]
                if ttg == 0:
                    depsd.append((E_v, v_ccevac))
                if ttg == 1:
                    depsd.append((E_v, v_vev.get(NT - 1)))
                first = True
                for hc in range(NHC):
                    for hf in range(2):
                        last = (hc == NHC - 1 and hf == 1)
                        op("tensor", depsd if first else [], E_t if last else None,
                           lambda e, hc=hc, hf=hf, tt=tt, P_dn=P_dn: e.matmul(
                               P_dn.ap()[:, hf * 512:(hf + 1) * 512],
                               bigW[:, hc * TG + tt * 128: hc * TG + (tt + 1) * 128],
                               wd_sb[:, hc * 1024 + hf * 512: hc * 1024 + (hf + 1) * 512],
                               start=(hc == 0), stop=(hc == NHC - 1)))
                        first = False
                t_down[ttg] = E_t.n
                down_last = E_t.n
                depsa = [(E_v, v_kvbd), (E_sc, sc_qsoft[ttg]),
                         (E_v, v_ep1.get(ttg - 1)), (E_v, v_ep3.get(ttg - 1))]
                if ttg == 0:
                    depsa += [(E_v, v_vev.get(NT - 1)), (E_v, v_kcopy.get(NT - 1))]
                first = True
                for p in range(8):
                    op("tensor", depsa if first else [], None,
                       lambda e, p=p, tpar=tpar: e.matmul(
                           P_rA.ap()[:, p * 128:(p + 1) * 128],
                           qT_t[tpar][:, p * 128:(p + 1) * 128],
                           kv_bd[:, p * 130: p * 130 + 128],
                           start=(p % 4 == 0), stop=(p % 4 == 3)))
                    first = False
                    op("tensor", [], E_t if p == 7 else None,
                       lambda e, p=p, tpar=tpar, P_de=P_de: e.matmul(
                           P_de.ap()[:, 2 * p: 2 * p + 2],
                           qT_t[tpar][:, p * 128:(p + 1) * 128],
                           kv_bd[:, p * 130 + 128: p * 130 + 130],
                           start=(p == 0), stop=(p == 7)))
                t_attn[ttg] = E_t.n
                epdeps = [(E_t, E_t.n), (E_v, v_ep1.get(ttg - 2))]
                if ttg == 0:
                    epdeps += [(E_v, v_qhat.get(NT - 1)), (E_v, v_kmul.get(NT - 1)),
                               (E_sc, sc_sq.get(NT - 1)), (E_sc, sc_wq), (E_v, v_wq)]
                op("vector", epdeps, E_v,
                   lambda e, tpar=tpar, P_de=P_de: e.reciprocal(rden[tpar][:], P_de.ap()[:, 0:16]))
                v_ep0[ttg] = E_v.n
                op("vector", [(E_v, E_v.n)], E_v,
                   lambda e, tpar=tpar: e.tensor_tensor(
                       ep_scr1.rearrange("p (a b c) -> p a b c", a=8, b=2),
                       P_rA.ap().rearrange("p (a b c) -> p a b c", a=8, b=2),
                       rden[tpar][:].rearrange("p (a b) -> p a b", a=8)
                       .unsqueeze(3).broadcast_to([128, 8, 2, 64]),
                       ALU.mult))
                v_ep1[ttg] = E_v.n
                ep2_deps = [(E_v, E_v.n), s_x3[ttg]]
                if ttg == 0:
                    ep2_deps.append(cv_out_guard.get(7))
                op("vector", [d for d in ep2_deps if d], E_v,
                   lambda e, tpar=tpar: e.tensor_add(ep_scr2, ep_scr1, xb[tpar]))
                v_ep2[ttg] = E_v.n
                ep3deps = [(E_v, E_v.n), (E_t, t_down[ttg])] + (
                    [s_out[ttg - 2]] if ttg >= 2 else [])
                if ttg == 1:
                    ep3deps.append((E_v, v_kvbd))
                op("vector", ep3deps, E_v,
                   lambda e, tpar=tpar, P_dn=P_dn: e.tensor_add(outb[tpar][:], ep_scr2, P_dn.ap()[:]))
                v_ep3[ttg] = E_v.n
        op("sync", [(E_v, v_ep3[NT - 1])], E_out[(NT - 1) % 2],
           lambda e, tg=NT - 1: e.dma_start(
               out=y_out[tg * 128:(tg + 1) * 128, :], in_=outb[tg % 2][:]), k=16)
        s_out[NT - 1] = (E_out[(NT - 1) % 2], E_out[(NT - 1) % 2].n)

        if debug_taps:
            dbg_deps = ([(E_qt[0], E_qt[0].n), (E_qt[1], E_qt[1].n),
                         (E_v, E_v.n), (E_sc, E_sc.n)])
            for t in range(NT):
                op("gpsimd", dbg_deps if t == 0 else [], E_gd,
                   lambda e, t=t: e.dma_start(out=dbg_qt[t], in_=qT_scr[t]), k=16)
            op("gpsimd", [], E_gd, lambda e: e.dma_start(out=dbg_wd, in_=wd_sb[:]), k=16)
            op("gpsimd", [], E_gd, lambda e: e.dma_start(out=dbg_h, in_=bigW[:, 0:NHC * 512]), k=16)
            op("gpsimd", [], E_gd, lambda e: e.dma_start(out=dbg_kvbd, in_=kv_bd[:]), k=16)
            op("gpsimd", [(E_gd, E_gd.n)], None, lambda e: e.nop())
        op("sync", [(E_out[0], E_out[0].n), (E_out[1], E_out[1].n)], None,
           lambda e: e.nop())

        block = es.enter_context(nc.Block())

        @block.sync
        def _(sync):
            for th in P.q["sync"]:
                th(sync)

        @block.vector
        def _(vector):
            for th in P.q["vector"]:
                th(vector)

        @block.scalar
        def _(scalar):
            for th in P.q["scalar"]:
                th(scalar)

        @block.tensor
        def _(tensor):
            for th in P.q["tensor"]:
                th(tensor)

        @block.gpsimd
        def _(gpsimd):
            for th in P.q["gpsimd"]:
                th(gpsimd)

    return nc


def _prep_inputs(L0_attn_x, L0_attn_W_qkv, L0_ffn_w_gate, L0_ffn_w_up, L0_ffn_w_down):
    x = np.asarray(L0_attn_x, dtype=np.float32)
    wq = np.ascontiguousarray(np.asarray(L0_attn_W_qkv, dtype=np.float32).T)
    wg = np.ascontiguousarray(
        np.asarray(L0_ffn_w_gate, dtype=np.int32).T.reshape(NDC, 128, NHC, 128)
        .transpose(2, 1, 0, 3).reshape(NHC, 128, D))
    wu = np.ascontiguousarray(
        np.asarray(L0_ffn_w_up, dtype=np.int32).T.reshape(NDC, 128, NHC, 128)
        .transpose(2, 1, 0, 3).reshape(NHC, 128, D))
    wd = np.ascontiguousarray(np.asarray(L0_ffn_w_down, dtype=np.int32).T)
    ident = np.eye(128, dtype=ml_dtypes.bfloat16)
    b, s, d = x.shape
    half = s // 2
    shards = []
    for c in range(8):
        sh = np.ascontiguousarray(x[c // 2, (c % 2) * half:(c % 2 + 1) * half, :])
        shards.append({"x": sh, "wqkv": wq, "wg": wg, "wu": wu, "wd": wd, "ident": ident})
    return shards, (b, s, d)


def kernel(L0_attn_x, L0_attn_W_qkv, L0_ffn_w_gate, L0_ffn_w_up, L0_ffn_w_down):
    in_maps, (b, s, d) = _prep_inputs(L0_attn_x, L0_attn_W_qkv, L0_ffn_w_gate,
                                      L0_ffn_w_up, L0_ffn_w_down)
    nc = build_program(TCORE=s // 2)
    res = run_bass_kernel_spmd(nc, in_maps, core_ids=list(range(8)))
    out = np.empty((b, s, d), dtype=np.float32)
    half = s // 2
    for c in range(8):
        out[c // 2, (c % 2) * half:(c % 2 + 1) * half, :] = res.results[c]["y"]
    return out


# version 25
# speedup vs baseline: 1.0483x; 1.0483x over previous
import contextlib
import numpy as np
import ml_dtypes
import concourse.bass as bass
import concourse.mybir as mybir
from concourse.bass_utils import run_bass_kernel_spmd

F32 = mybir.dt.float32
BF16 = mybir.dt.bfloat16
I32 = mybir.dt.int32
AF = mybir.ActivationFunctionType
ALU = mybir.AluOpType
AX = mybir.AxisListType

D = 1024
H = 16
HD = 64
FF = 3072
NHC = 24
NDC = 8
EPS_LN = 1e-5
EPS_ATT = 1e-6
EPS_L2 = 1e-12
SCALE_ATT = float(64 ** -0.5)
DEQ = 1.0 / 127.0

ENGS = ("sync", "vector", "scalar", "tensor", "gpsimd")


class Edge:
    def __init__(self, es, nc, name):
        self.h = es.enter_context(nc.semaphore(name))
        self.n = 0


class Plan:
    def __init__(self):
        self.q = {e: [] for e in ENGS}
        self.lastw = {e: {} for e in ENGS}

    def op(self, eng, deps, inc, build, k=1):
        dd = []
        for ed, v in deps:
            if ed is None or v is None or v <= 0:
                continue
            key = id(ed)
            if self.lastw[eng].get(key, 0) >= v:
                continue
            self.lastw[eng][key] = v
            dd.append((ed.h, v))
        icap = None
        if inc is not None:
            inc.n += k
            icap = (inc.h, k)

        def thunk(eo, dd=tuple(dd), icap=icap, build=build):
            for h, v in dd:
                eo.wait_ge(h, v)
            ins = build(eo)
            if icap is not None:
                ins.then_inc(icap[0], icap[1])

        self.q[eng].append(thunk)
        return inc.n if inc is not None else None


def build_program(TCORE=4096, detect_races=False, debug_taps=False):
    NT = TCORE // 128
    TG = 512
    NTG = TCORE // TG
    NTT = TG // 128

    nc = bass.Bass("TRN2", target_bir_lowering=False, debug=False,
                   detect_race_conditions=detect_races)

    x_in = nc.dram_tensor("x", [TCORE, D], F32, kind="ExternalInput").ap()
    wqkv = nc.dram_tensor("wqkv", [D, 3 * D], F32, kind="ExternalInput").ap()
    wg_i = nc.dram_tensor("wg", [NHC, 128, D], I32, kind="ExternalInput").ap()
    wu_i = nc.dram_tensor("wu", [NHC, 128, D], I32, kind="ExternalInput").ap()
    wd_i = nc.dram_tensor("wd", [FF, D], I32, kind="ExternalInput").ap()
    ident_in = nc.dram_tensor("ident", [128, 128], BF16, kind="ExternalInput").ap()
    y_out = nc.dram_tensor("y", [TCORE, D], F32, kind="ExternalOutput").ap()

    wg_bf = nc.dram_tensor("wg_bf", [NHC, 128, 1024], BF16).ap()
    wu_bf = nc.dram_tensor("wu_bf", [NHC, 128, 1024], BF16).ap()
    qT_scr = nc.dram_tensor("qT_scr", [NT, 128, 1024], BF16).ap()
    xnT_scr = nc.dram_tensor("xnT_scr", [NT, 128, 1024], BF16).ap()
    cc_in = nc.dram_tensor("cc_in", [128, 520], F32)
    cc_out = nc.dram_tensor("cc_out", [128, 520], F32)
    dbg_qt = dbg_wd = dbg_h = dbg_kvbd = None
    if debug_taps:
        dbg_qt = nc.dram_tensor("dbg_qt", [NT, 128, 1024], BF16, kind="ExternalOutput").ap()
        dbg_wd = nc.dram_tensor("dbg_wd", [128, NHC * 1024], BF16, kind="ExternalOutput").ap()
        dbg_h = nc.dram_tensor("dbg_h", [128, NHC * 512], BF16, kind="ExternalOutput").ap()
        dbg_kvbd = nc.dram_tensor("dbg_kvbd", [128, 8 * 130], BF16, kind="ExternalOutput").ap()

    es = contextlib.ExitStack()
    with es:
        def sb(name, shape, dt):
            return es.enter_context(nc.sbuf_tensor(name, shape, dt))

        xb4 = sb("xb4", [128, 4 * D], F32)
        xb = [xb4.ap()[:, i * D:(i + 1) * D] for i in range(4)]
        qn = sb("qn", [128, D], F32)
        kn = sb("kn", [128, D], F32)
        eab = sb("eab", [128, D], BF16)
        ep2b = sb("ep2b", [128, D], F32)
        outb = [sb("outb0", [128, D], F32), sb("outb1", [128, D], F32)]
        sq_k = sb("sq_k", [128, D], BF16)
        xnb = [sb("xnb0", [128, D], BF16), sb("xnb1", [128, D], BF16)]
        qhat = [sb("qhat0", [128, D], BF16), sb("qhat1", [128, D], BF16)]
        khat = [sb(f"khat{i}", [128, D], BF16) for i in range(4)]
        vtile = [sb(f"vt{i}", [128, H * 65], BF16) for i in range(4)]
        xnT_t = [sb("xnTt0", [128, D], BF16), sb("xnTt1", [128, D], BF16),
                 sb("xnTt2", [128, D], BF16), sb("xnTt3", [128, D], BF16)]
        qh_st = [sb("qhst0", [128, D], BF16), sb("qhst1", [128, D], BF16)]
        wcast = [sb("wc0", [128, 1024], BF16), sb("wc1", [128, 1024], BF16)]
        xnT_g = [sb("xnTg0", [128, NDC * TG], BF16), sb("xnTg1", [128, NDC * TG], BF16)]
        _xbt = xb4.ap()[:, 2 * D:4 * D].bitcast(BF16)
        wg_sb = [_xbt[:, 0:1024], _xbt[:, 1024:2048]]
        wu_sb = [_xbt[:, 2048:3072], _xbt[:, 3072:4096]]
        wd_sb = sb("wd_sb", [128, NHC * 1024], BF16)
        bigW = sb("bigW", [128, NDC * 3 * D], BF16)
        kv_bd = sb("kv_bd", [128, 8 * 130], BF16)
        ident = sb("ident_sb", [128, 128], BF16)
        wi = [sb("wi0", [128, 1024], I32), sb("wi1", [128, 1024], I32)]
        qT_t = [wi[0].ap().bitcast(BF16)[:, 0:1024], wi[1].ap().bitcast(BF16)[:, 0:1024]]
        cst = sb("cst", [128, 3], F32)
        ssum = sb("ssum", [128, 4], F32)
        ssq = sb("ssq", [128, 4], F32)
        mb = sb("mb", [128, 4], F32)
        mm2 = sb("mm2", [128, 4], F32)
        varb = sb("varb", [128, 4], F32)
        stdv = sb("stdv", [128, 4], F32)
        rstd = sb("rstd", [128, 4], F32)
        negmr = sb("negmr", [128, 4], F32)
        ssq_q = sb("ssq_q", [128, H], F32)
        ssq_k = sb("ssq_k", [128, H], F32)
        nrm2_q = sb("nrm2_q", [128, H], F32)
        nrm2_k = sb("nrm2_k", [128, H], F32)
        rst_q = sb("rst_q", [128, H], F32)
        rst_k = sb("rst_k", [128, H], F32)
        rden = [sb("rden0", [128, H], F32), sb("rden1", [128, H], F32)]

        sqdump = eab.ap()[:, 0:D]
        cc_sb = outb[1].ap()[:, 0:520]
        sg = [kn.ap()[:, 0:512], kn.ap()[:, 512:1024]]
        ep_scr1 = qn.ap()[:, 0:D]
        ep_scr2 = ep2b.ap()[:]
        P_rA = es.enter_context(nc.psum_tensor("P_rA", [128, 1024], F32))
        P_rB = es.enter_context(nc.psum_tensor("P_rB", [128, 1024], F32))
        P_kv = es.enter_context(nc.psum_tensor("P_kv", [128, 1024], F32))
        P_m1 = es.enter_context(nc.psum_tensor("P_m1", [128, 512], F32))
        P_m2 = es.enter_context(nc.psum_tensor("P_m2", [128, 512], F32))
        P_m1b = P_m1.ap().bitcast(BF16)
        P_m2b = P_m2.ap().bitcast(BF16)

        E_gd = Edge(es, nc, "E_gd")
        E_t = Edge(es, nc, "E_t")
        E_v = Edge(es, nc, "E_v")
        E_sc = Edge(es, nc, "E_sc")
        E_cc = Edge(es, nc, "E_cc")
        E_x = [Edge(es, nc, f"E_x{i}") for i in range(4)]
        E_cvl = [Edge(es, nc, f"E_cvl{i}") for i in range(8)]
        E_cvst = [Edge(es, nc, f"E_cvst{i}") for i in range(8)]
        E_xnt = [Edge(es, nc, f"E_xnt{i}") for i in range(4)]
        E_xnl = [Edge(es, nc, f"E_xnl{i}") for i in range(4)]
        E_qt = [Edge(es, nc, f"E_qt{i}") for i in range(2)]
        E_xg = [Edge(es, nc, f"E_xg{i}") for i in range(2)]
        E_wg = [Edge(es, nc, f"E_wg{i}") for i in range(2)]
        E_wu = [Edge(es, nc, f"E_wu{i}") for i in range(2)]
        E_x3 = [Edge(es, nc, f"E_x3{i}") for i in range(2)]
        E_out = [Edge(es, nc, f"E_out{i}") for i in range(2)]
        E_wq = [Edge(es, nc, f"E_wq{i}") for i in range(2)]
        E_qtl = [Edge(es, nc, f"E_qtl{i}") for i in range(2)]
        E_ccin = Edge(es, nc, "E_ccin")
        E_ccout = Edge(es, nc, "E_ccout")
        E_misc = Edge(es, nc, "E_misc")

        P = Plan()
        op = P.op

        s_x = {}; sc_sq = {}; v_var = {}; sc_sqrt = {}; v_stats = {}; sc_xn = {}
        t_trx = {}; v_evacX = {}; s_xnt = {}
        t_q = {}; sc_qsq = {}; v_gq = {}; sc_nq = {}; v_qmul = {}; sc_expq = {}; sc_qhat = {}
        t_k = {}; sc_ksq = {}; v_gk = {}; sc_nk = {}; v_kmul = {}; sc_expk = {}; sc_khat = {}
        t_v = {}; v_vev = {}; t_kv = {}; t_trq = {}; v_evacQ = {}; s_qt = {}
        gd_wg = {}; sc_cwg = {}; s_swg = {}
        gd_wu = {}; v_cwu = {}; s_swu = {}
        gd_wd = {}; sc_cwd = {}
        t_gate = {}; sc_silu = {}; t_up = {}; v_hmul = {}
        s_w = {}; s_xg = {}; s_x3 = {}; sd_qt = {}
        t_down = {}; t_attn = {}
        v_ep0 = {}; v_ep1 = {}; v_ep2 = {}; v_ep3 = {}; s_out = {}

        op("sync", [], E_misc, lambda e: e.dma_start(out=ident[:], in_=ident_in), k=16)
        s_ident = (E_misc, E_misc.n)
        for _vt in vtile:
            op("vector", [], E_v, lambda e, _vt=_vt: e.memset(_vt[:], 1.0))
        op("vector", [], E_v, lambda e: e.memset(kv_bd[:], 0.0))
        op("vector", [], E_v, lambda e: e.memset(cst[:, 0:1], EPS_LN))
        op("vector", [], E_v, lambda e: e.memset(cst[:, 1:2], 1.0))
        op("vector", [], E_v, lambda e: e.memset(cst[:, 2:3], EPS_L2 * EPS_L2))
        v_init = E_v.n

        wq_cast = {}
        wq_sc_last = [0]
        wq_v_last = [0]

        wq_ld = {}

        def wq_load(kk):
            if kk in wq_ld:
                return
            if kk >= 2 and (kk - 2) not in wq_cast:
                return
            dc, s = divmod(kk, 3)
            par = kk % 2
            deps = []
            if kk >= 2:
                pe, pv = wq_cast[kk - 2]
                deps = [((E_sc if pe == "scalar" else E_v), pv)]
            op("sync", deps, E_wq[par],
               lambda e, dc=dc, s=s, par=par: e.dma_start(
                   out=outb[par][:], in_=wqkv[dc * 128:(dc + 1) * 128, s * 1024:(s + 1) * 1024]), k=16)
            wq_ld[kk] = (E_wq[par], E_wq[par].n)

        def wq_cast_task(kk):
            wq_load(kk)
            par = kk % 2
            dc, s = divmod(kk, 3)
            eng = "scalar" if (kk % 2 == 0) else "vector"
            ed = E_sc if eng == "scalar" else E_v
            col = dc * 3 * D + s * 1024
            if eng == "scalar":
                op(eng, [wq_ld[kk]], ed,
                   lambda e, par=par, col=col: e.activation(
                       bigW[:, col:col + 1024], outb[par][:], AF.Copy))
                wq_sc_last[0] = ed.n
            else:
                op(eng, [wq_ld[kk]], ed,
                   lambda e, par=par, col=col: e.tensor_copy(bigW[:, col:col + 1024], outb[par][:]))
                wq_v_last[0] = ed.n
            wq_cast[kk] = (eng, ed.n)

        def conv_sched(t):
            return range(t * NHC // NT, (t + 1) * NHC // NT)

        sc_qcopy = {}
        v_kcopy = {}
        v_xn = {}
        sc_evacQ = {}
        sc_qsoft = {}
        v_qhat = {}
        sc_khat = {}
        v_gq = {}
        v_gk = {}
        NP = NT // 2

        def front_x(t):
            op("sync", [(E_v, v_xn.get(t - 4)), (E_sc, sc_sq.get(t - 4))], E_x[t % 4],
               lambda e, t=t: e.dma_start(out=xb[t % 4], in_=x_in[t * 128:(t + 1) * 128, :]), k=16)
            s_x[t] = (E_x[t % 4], E_x[t % 4].n)

        def round_mm(t, base, Pr, deps):
            tpar = t % 4
            for dc in range(NDC):
                for hf in range(2):
                    ed = E_t if (dc == NDC - 1 and hf == 1) else None
                    op("tensor", deps if (dc == 0 and hf == 0) else [], ed,
                       lambda e, dc=dc, hf=hf, tpar=tpar, Pr=Pr, base=base: e.matmul(
                           Pr.ap()[:, hf * 512:(hf + 1) * 512],
                           xnT_t[tpar][:, dc * 128:(dc + 1) * 128],
                           bigW[:, dc * 3 * D + base + hf * 512: dc * 3 * D + base + (hf + 1) * 512],
                           start=(dc == 0), stop=(dc == NDC - 1)))
            return E_t.n

        def kv_col(p):
            return (p // 4) * 512 + (p % 4) * 65

        def backT_kv(t):
            deps = [(E_sc, sc_khat[t]), (E_v, v_vev[t])]
            first = True
            for p in range(8):
                for eh in range(2):
                    h = 2 * p + eh
                    st = (t == 0) and (p % 4 == 0)
                    sp = (t == NT - 1) and (p % 4 == 3)
                    last = (p == 7) and (eh == 1)
                    op("tensor", deps if first else [], E_t if last else None,
                       lambda e, p=p, eh=eh, h=h, st=st, sp=sp, t=t: e.matmul(
                           P_kv.ap()[eh * 64:(eh + 1) * 64, kv_col(p):kv_col(p) + 65],
                           khat[t % 4][:, h * 64:(h + 1) * 64],
                           vtile[t % 4][:, h * 65:(h + 1) * 65],
                           start=st, stop=sp, skip_group_check=True))
                    first = False
            t_kv[t] = E_t.n

        cv_in = [wi[0].ap(), wi[1].ap(),
                 qn.ap().bitcast(I32), kn.ap().bitcast(I32),
                 xnT_g[0].ap().bitcast(I32)[:, 0:1024],
                 xnT_g[0].ap().bitcast(I32)[:, 1024:2048],
                 xnT_g[1].ap().bitcast(I32)[:, 0:1024],
                 xnT_g[1].ap().bitcast(I32)[:, 1024:2048]]
        cv_out = [wcast[0].ap()[:], wcast[1].ap()[:], qhat[0].ap()[:], khat[0].ap()[:],
                  qh_st[0].ap()[:], qh_st[1].ap()[:], sq_k.ap()[:],
                  ep2b.ap().bitcast(BF16)[:, 0:1024]]
        NSTG = 8
        cv_ld = {}
        cv_cast = {}
        cv_st = {}
        cv_tasks = []
        for hc in range(NHC):
            cv_tasks += [(0, hc), (1, hc), (2, hc)]

        def conv_store(j):
            kind, hc = cv_tasks[j]
            if kind == 2:
                return
            b = j % NSTG
            pe, pv = cv_cast[j]
            dstd = wg_bf[hc] if kind == 0 else wu_bf[hc]
            op("scalar", [((E_sc if pe == "scalar" else E_v), pv)], E_cvst[b],
               lambda e, dstd=dstd, b=b: e.dma_start(out=dstd, in_=cv_out[b]), k=16)
            cv_st[j] = (E_cvst[b], E_cvst[b].n)

        def conv_task(k):
            kind, hc = cv_tasks[k]
            b = k % NSTG
            deps = []
            if k >= NSTG:
                pe, pv = cv_cast[k - NSTG]
                deps.append(((E_sc if pe == "scalar" else E_v), pv))
                if cv_st[k - NSTG] is not None:
                    deps.append(cv_st[k - NSTG])
            if kind == 0:
                op("gpsimd", deps, E_cvl[b],
                   lambda e, hc=hc, b=b: e.dma_start(out=cv_in[b], in_=wg_i[hc]), k=16)
            elif kind == 1:
                op("gpsimd", deps, E_cvl[b],
                   lambda e, hc=hc, b=b: e.dma_start(out=cv_in[b], in_=wu_i[hc]), k=16)
            else:
                op("gpsimd", deps, E_cvl[b],
                   lambda e, hc=hc, b=b: e.dma_start(
                       out=cv_in[b], in_=wd_i[hc * 128:(hc + 1) * 128, :]), k=16)
            cv_ld[k] = (E_cvl[b], E_cvl[b].n)
            eng = "scalar" if (k % 2 == 0) else "vector"
            ed = E_sc if eng == "scalar" else E_v
            if kind == 2:
                dst = wd_sb.ap()[:, hc * 1024:(hc + 1) * 1024]
            else:
                dst = cv_out[b]
            if eng == "scalar":
                op(eng, [cv_ld[k]], ed,
                   lambda e, b=b, dst=dst: e.activation(dst, cv_in[b], AF.Copy, scale=DEQ))
            else:
                op(eng, [cv_ld[k]], ed,
                   lambda e, b=b, dst=dst: e.tensor_scalar_mul(dst, cv_in[b], DEQ))
            cv_cast[k] = (eng, ed.n)
            if kind == 2:
                cv_st[k] = None
                sc_cwd[hc] = ed.n if eng == "scalar" else None
                g_conv[hc] = (E_sc if eng == "scalar" else E_v, ed.n)
            else:
                conv_store(k)

        def conv(tp):
            k0 = tp * len(cv_tasks) // NP
            k1 = (tp + 1) * len(cv_tasks) // NP
            for k in range(k0, k1):
                conv_task(k)

        g_conv = {}

        qb = [qn.ap().bitcast(BF16)[:, 0:1024], qn.ap().bitcast(BF16)[:, 1024:2048]]
        kb = [kn.ap().bitcast(BF16)[:, 0:1024], kn.ap().bitcast(BF16)[:, 1024:2048]]

        def ladder(t, which):
            par = t % 2
            Pr = ((P_rA if t % 2 == 0 else P_rB) if which == 0
                  else (P_rB if t % 2 == 0 else P_rA))
            if which == 0:
                qc_deps = [(E_t, t_q[t]), (E_v, v_qhat.get(t - 2))]
                if t == 0:
                    qc_deps.append(cv_stage_guard.get(2))
                op("scalar", [d for d in qc_deps if d], E_sc,
                   lambda e, Pr=Pr, par=par: e.copy(qb[par], Pr.ap()[:]))
                sc_qcopy[t] = E_sc.n
                op("scalar", [(E_sc, E_sc.n), (E_v, v_gq.get(t - 1))], E_sc,
                   lambda e, par=par: e.activation(eab[:], qb[par], AF.Square))
                op("vector", [(E_sc, E_sc.n)], E_v,
                   lambda e: e.reduce_sum(
                       ssq_q[:], eab[:].rearrange("p (h d) -> p h d", h=H), axis=AX.X))
                v_gq[t] = E_v.n
                op("scalar", [(E_v, E_v.n)], E_sc,
                   lambda e: e.activation(nrm2_q[:], ssq_q[:], AF.Ln, bias=cst[:, 2:3]))
                op("scalar", [(E_sc, E_sc.n)], E_sc,
                   lambda e: e.activation(rst_q[:], nrm2_q[:], AF.Exp, scale=-0.5))
                qm_deps = [(E_sc, E_sc.n), (E_t, t_trq.get(t - 2))]
                if t == 0:
                    qm_deps.append(cv_out_guard.get(2))
                op("vector", [d for d in qm_deps if d], E_v,
                   lambda e, par=par: e.tensor_tensor(
                       qhat[par][:].rearrange("p (h d) -> p h d", h=H),
                       qb[par].rearrange("p (h d) -> p h d", h=H),
                       rst_q[:].unsqueeze(2).broadcast_to([128, H, HD]),
                       ALU.mult))
                v_qhat[t] = E_v.n
            else:
                kc_deps = [(E_t, t_k[t]), (E_sc, sc_khat.get(t - 2))]
                if t == 0:
                    kc_deps.append(cv_stage_guard.get(3))
                op("vector", [d for d in kc_deps if d], E_v,
                   lambda e, Pr=Pr, par=par: e.tensor_copy(kb[par], Pr.ap()[:]))
                v_kcopy[t] = E_v.n
                ks_deps = [(E_v, E_v.n), (E_v, v_gk.get(t - 1))]
                if t == 0:
                    ks_deps.append(cv_out_guard.get(6))
                op("scalar", [d for d in ks_deps if d], E_sc,
                   lambda e, par=par: e.activation(sq_k[:], kb[par], AF.Square))
                op("vector", [(E_sc, E_sc.n)], E_v,
                   lambda e: e.reduce_sum(
                       ssq_k[:], sq_k[:].rearrange("p (h d) -> p h d", h=H), axis=AX.X))
                v_gk[t] = E_v.n
                op("scalar", [(E_v, E_v.n)], E_sc,
                   lambda e: e.activation(nrm2_k[:], ssq_k[:], AF.Ln, bias=cst[:, 2:3]))
                op("scalar", [(E_sc, E_sc.n)], E_sc,
                   lambda e: e.activation(rst_k[:], nrm2_k[:], AF.Exp, scale=-0.5))
                op("vector", [(E_sc, E_sc.n)], E_v,
                   lambda e, par=par: e.tensor_tensor(
                       eab[:].rearrange("p (h d) -> p h d", h=H),
                       kb[par].rearrange("p (h d) -> p h d", h=H),
                       rst_k[:].unsqueeze(2).broadcast_to([128, H, HD]),
                       ALU.mult))
                v_kmul[t] = E_v.n
                op("scalar", [(E_v, E_v.n)], E_sc,
                   lambda e: e.activation(sq_k[:], eab[:], AF.Exp))
                lk_deps = [(E_sc, E_sc.n), (E_t, t_kv.get(t - 4))]
                if t == 0:
                    lk_deps.append(cv_out_guard.get(3))
                op("scalar", [d for d in lk_deps if d], E_sc,
                   lambda e, t=t: e.activation(khat[t % 4][:], sq_k[:], AF.Ln, bias=cst[:, 1:2]))
                sc_khat[t] = E_sc.n

        def back_tile(t):
            t_q[t] = round_mm(t, 0, P_rA if t % 2 == 0 else P_rB,
                              [s_xnl[t], (E_v, v_kcopy.get(t - 1)),
                               (E_sc, sc_wq), (E_v, v_wq)])
            ladder(t, 0)
            t_k[t] = round_mm(t, D, P_rB if t % 2 == 0 else P_rA,
                              [(E_v, v_vev.get(t - 1))])
            ladder(t, 1)
            t_v[t] = round_mm(t, 2 * D, P_rA if t % 2 == 0 else P_rB,
                              [(E_sc, sc_qcopy[t])])
            op("vector", [(E_t, t_v[t]), (E_t, t_kv.get(t - 4)), (E_v, v_init)], E_v,
               lambda e, t=t: e.tensor_copy(
                   vtile[t % 4][:].rearrange("p (h c) -> p h c", h=H)[:, :, 0:64],
                   (P_rA if t % 2 == 0 else P_rB).ap().rearrange("p (h d) -> p h d", h=H)))
            v_vev[t] = E_v.n

        def trq_block(t):
            Pm = P_m1b if t % 2 == 0 else P_m2b
            depsq = [(E_v, v_qhat[t]), (E_sc, sc_evacQ.get(t - 2))]
            if t < 2:
                depsq.append((E_v, v_evacX.get(NT - 1)))
            for dc in range(NDC):
                op("tensor", depsq if dc == 0 else [],
                   E_t if dc == NDC - 1 else None,
                   lambda e, dc=dc, t=t, Pm=Pm: e.transpose(
                       Pm[:, dc * 128:(dc + 1) * 128],
                       qhat[t % 2][:, dc * 128:(dc + 1) * 128], ident[:]))
            t_trq[t] = E_t.n
            eq_deps = [(E_t, E_t.n)] + ([s_qt[t - 2]] if t >= 2 else [])
            if t < 2:
                eq_deps.append(cv_out_guard.get(4 + t))
            op("scalar", [d for d in eq_deps if d], E_sc,
               lambda e, t=t, Pm=Pm: e.copy(qh_st[t % 2][:], Pm[:, 0:1024]))
            sc_evacQ[t] = E_sc.n
            op("sync", [(E_sc, E_sc.n)], E_qt[t % 2],
               lambda e, t=t: e.dma_start(out=qT_scr[t], in_=qh_st[t % 2][:]), k=16)
            s_qt[t] = (E_qt[t % 2], E_qt[t % 2].n)

        def front_tile(t, i, fp):
            sl = 2 * (fp % 2)
            op("vector", [(E_sc, sc_sqrt[fp]), (E_v, v_stats[fp]),
                          (E_t, t_trx.get(t - 2))], E_v,
               lambda e, i=i, t=t, sl=sl: e.tensor_scalar(
                   xnb[t % 2][:], xb[t % 4], negmr[:, sl + i:sl + i + 1], rstd[:, sl + i:sl + i + 1],
                   ALU.add, ALU.mult))
            v_xn[t] = E_v.n
            Pm = P_m1b if t % 2 == 0 else P_m2b
            depsx = [(E_v, v_xn[t]), (E_v, v_evacX.get(t - 2)), s_ident]
            for dc in range(NDC):
                op("tensor", depsx if dc == 0 else [],
                   E_t if dc == NDC - 1 else None,
                   lambda e, dc=dc, t=t, Pm=Pm: e.transpose(
                       Pm[:, dc * 128:(dc + 1) * 128],
                       xnb[t % 2][:, dc * 128:(dc + 1) * 128], ident[:]))
            t_trx[t] = E_t.n
            op("vector", [(E_t, E_t.n)] + ([s_xnt[t - 4]] if t >= 4 else []), E_v,
               lambda e, t=t, Pm=Pm: e.tensor_copy(xnT_t[t % 4][:], Pm[:, 0:1024]))
            v_evacX[t] = E_v.n
            op("sync", [(E_v, v_evacX[t])], E_xnt[t % 4],
               lambda e, t=t: e.dma_start(out=xnT_scr[t], in_=xnT_t[t % 4][:]), k=16)
            s_xnt[t] = (E_xnt[t % 4], E_xnt[t % 4].n)

        front_x(0)
        front_x(1)
        wq_n = NDC * 3
        wq_np = min(NP, 12)

        def wq_range(fp):
            if fp >= wq_np:
                return range(0)
            return range(fp * wq_n // wq_np, (fp + 1) * wq_n // wq_np)

        for fp in range(NP):
            if fp + 1 < NP:
                front_x(2 * fp + 2)
                front_x(2 * fp + 3)
            for kk in wq_range(fp):
                wq_load(kk)
            if True:
                a, b = 2 * fp, 2 * fp + 1
                sl = 2 * (fp % 2)
                op("scalar", [s_x[a], (E_v, v_stats.get(fp - 2)), (E_v, v_var.get(fp - 2))], E_sc,
                   lambda e, a=a, sl=sl: e.activation(sqdump, xb[a % 4], AF.Square, accum_out=ssq[:, sl:sl + 1]))
                op("scalar", [s_x[b]], E_sc,
                   lambda e, b=b, sl=sl: e.activation(sqdump, xb[b % 4], AF.Square, accum_out=ssq[:, sl + 1:sl + 2]))
                sc_sq[a] = sc_sq[b] = E_sc.n
                op("vector", [s_x[a], s_x[b], (E_v, v_stats.get(fp - 2))], E_v,
                   lambda e, a=a, sl=sl: e.reduce_sum(
                       ssum[:, sl:sl + 2], xb4.ap()[:, (a % 4) * D:(a % 4 + 2) * D]
                       .rearrange("p (i d) -> p i d", i=2), axis=AX.X))
                op("vector", [(E_v, E_v.n)], E_v,
                   lambda e, sl=sl: e.tensor_scalar_mul(mb[:, sl:sl + 2], ssum[:, sl:sl + 2], 1.0 / D))
                op("vector", [(E_v, E_v.n)], E_v,
                   lambda e, sl=sl: e.tensor_mul(mm2[:, sl:sl + 2], mb[:, sl:sl + 2], mb[:, sl:sl + 2]))
                op("vector", [(E_v, E_v.n), (E_sc, sc_sq[b])], E_v,
                   lambda e, sl=sl: e.tensor_scalar(varb[:, sl:sl + 2], ssq[:, sl:sl + 2], 1.0 / D, EPS_LN, ALU.mult, ALU.add))
                op("vector", [(E_v, E_v.n)], E_v,
                   lambda e, sl=sl: e.tensor_sub(varb[:, sl:sl + 2], varb[:, sl:sl + 2], mm2[:, sl:sl + 2]))
                v_var[fp] = E_v.n
                op("scalar", [(E_v, E_v.n)], E_sc,
                   lambda e, sl=sl: e.activation(stdv[:, sl:sl + 2], varb[:, sl:sl + 2], AF.Ln))
                op("scalar", [(E_sc, E_sc.n)], E_sc,
                   lambda e, sl=sl: e.activation(rstd[:, sl:sl + 2], stdv[:, sl:sl + 2], AF.Exp, scale=-0.5))
                sc_sqrt[fp] = E_sc.n
                op("vector", [(E_v, v_var[fp])], E_v,
                   lambda e, sl=sl: e.tensor_scalar_mul(negmr[:, sl:sl + 2], mb[:, sl:sl + 2], -1.0))
                v_stats[fp] = E_v.n
                front_tile(a, 0, fp)
                front_tile(b, 1, fp)
            for kk in wq_range(fp):
                wq_cast_task(kk)
            conv(fp)
        sc_wq = wq_sc_last[0]
        v_wq = wq_v_last[0]

        cv_stage_guard = {}
        cv_out_guard = {}
        for k in range(len(cv_tasks)):
            b = k % NSTG
            pe, pv = cv_cast[k]
            cv_stage_guard[b] = ((E_sc if pe == "scalar" else E_v), pv)
            if cv_st.get(k) is not None:
                cv_out_guard[b] = cv_st[k]
        wd_guards = []
        _smax = _vmax = 0
        for k in range(len(cv_tasks)):
            if cv_tasks[k][0] == 2:
                pe, pv = cv_cast[k]
                if pe == "scalar":
                    _smax = max(_smax, pv)
                else:
                    _vmax = max(_vmax, pv)
        if _smax:
            wd_guards.append((E_sc, _smax))
        if _vmax:
            wd_guards.append((E_v, _vmax))

        s_xnl = {}

        def xnl(t):
            op("sync", [(E_t, t_v.get(t - 4)), (E_xnt[t % 4], E_xnt[t % 4].n)], E_xnl[t % 4],
               lambda e, t=t: e.dma_start(out=xnT_t[t % 4][:], in_=xnT_scr[t]), k=16)
            s_xnl[t] = (E_xnl[t % 4], E_xnl[t % 4].n)

        for ip in range(NP + 2):
            lp = ip if ip < NP else None
            bp = ip - 1 if 1 <= ip <= NP else None
            kp = ip - 2 if ip >= 2 else None
            if lp is not None:
                xnl(2 * lp)
                xnl(2 * lp + 1)
            if kp is not None:
                trq_block(2 * kp)
                trq_block(2 * kp + 1)
            if bp is not None:
                back_tile(2 * bp)
                back_tile(2 * bp + 1)
            if kp is not None:
                backT_kv(2 * kp)
                backT_kv(2 * kp + 1)
        op("vector", [(E_t, t_kv[NT - 1])], E_v,
           lambda e: e.tensor_copy(
               cc_sb.rearrange("p (b q c) -> p b q c", b=2, q=4),
               P_kv.ap().rearrange("p (b f) -> p b f", b=2)[:, :, 0:260]
               .rearrange("p b (q c) -> p b q c", q=4)))
        v_ccevac = E_v.n
        op("scalar", [(E_v, E_v.n)], E_ccin,
           lambda e: e.dma_start(out=cc_in.ap(), in_=cc_sb), k=16)
        op("gpsimd", [(E_ccin, 16)], E_cc,
           lambda e: e.collective_compute(
               "AllReduce", ALU.add,
               replica_groups=[[0, 1], [2, 3], [4, 5], [6, 7]],
               ins=[cc_in.ap().opt()], outs=[cc_out.ap().opt()]))

        ccv = cc_sb.rearrange("p (b q c) -> p b q c", b=2, q=4)

        def build_kvbd():
            deps = [(E_ccout, 16), (E_v, v_init)]
            first = True
            for p in range(8):
                for eh in range(2):
                    op("vector", deps if first else [], E_v,
                       lambda e, p=p, eh=eh: e.tensor_scalar_mul(
                           kv_bd[eh * 64:(eh + 1) * 64, p * 130 + eh * 64: p * 130 + eh * 64 + 64],
                           ccv[eh * 64:(eh + 1) * 64, p // 4, p % 4, 0:64], SCALE_ATT))
                    first = False
                    op("vector", [], E_v,
                       lambda e, p=p, eh=eh: e.tensor_scalar_add(
                           kv_bd[eh * 64:(eh + 1) * 64, p * 130 + 128 + eh: p * 130 + 129 + eh],
                           ccv[eh * 64:(eh + 1) * 64, p // 4, p % 4, 64:65], EPS_ATT))
            return E_v.n

        v_kvbd = None
        wg_rd = [None, None]
        wu_rd = [None, None]
        xg_rd = [None, None]
        sg_rd = [None, None]
        silu_last = None
        hmul_last = None
        down_last = None
        def emit_xg(g):
            gpar = g % 2
            depsg = [(ex, ex.n) for ex in E_xnt] + [(E_t, xg_rd[gpar])]
            if g < 2:
                depsg += [d for d in (cv_stage_guard.get(4 + 2 * g),
                                      cv_stage_guard.get(5 + 2 * g)) if d]
            first = True
            for tb in range(NTT):
                op("sync", depsg if first else [], E_xg[gpar],
                   lambda e, tb=tb, g=g, gpar=gpar: e.dma_start(
                       out=xnT_g[gpar][:].rearrange("p (dc tb c) -> p dc tb c", dc=NDC, tb=NTT)
                       [:, :, tb, :],
                       in_=xnT_scr[g * NTT + tb].rearrange("p (dc c) -> p dc c", dc=NDC)), k=16)
                first = False
            s_xg[g] = (E_xg[gpar], E_xg[gpar].n)

        for g in range(NTG):
            gpar = g % 2
            if g == 0:
                emit_xg(0)
            for hc in range(NHC):
                if hc == 12 and g + 1 < NTG:
                    emit_xg(g + 1)
                hpar = hc % 2
                if g == 0 and hc == 2:
                    op("scalar", [(E_cc, 1)], E_ccout,
                       lambda e: e.dma_start(out=cc_sb, in_=cc_out.ap()), k=16)
                wgl_deps = [(E_t, wg_rd[hpar])]
                wul_deps = [(E_t, wu_rd[hpar])]
                if g == 0 and hc < 2:
                    cvst_tot = [(ec, ec.n) for ec in E_cvst]
                    wgl_deps += cvst_tot + [(E_v, v_xn.get(NT - 1)), (E_sc, sc_sq.get(NT - 1))]
                    wul_deps += cvst_tot
                op("sync", wgl_deps, E_wg[hpar],
                   lambda e, hc=hc, hpar=hpar: e.dma_start(out=wg_sb[hpar], in_=wg_bf[hc]), k=16)
                op("sync", wul_deps, E_wu[hpar],
                   lambda e, hc=hc, hpar=hpar: e.dma_start(out=wu_sb[hpar], in_=wu_bf[hc]), k=16)
                s_w[(g, hc)] = [(E_wg[hpar], E_wg[hpar].n), (E_wu[hpar], E_wu[hpar].n)]
                depsgate = s_w[(g, hc)] + [s_xg[g],
                            (E_sc, silu_last),
                            (E_sc, sc_evacQ.get(NT - 2))] + wd_guards
                for dc in range(NDC):
                    op("tensor", depsgate if dc == 0 else [],
                       E_t if dc == NDC - 1 else None,
                       lambda e, dc=dc, hpar=hpar, gpar=gpar: e.matmul(
                           P_m1.ap()[:],
                           wg_sb[hpar][:, dc * 128:(dc + 1) * 128],
                           xnT_g[gpar][:, dc * TG:(dc + 1) * TG],
                           start=(dc == 0), stop=(dc == NDC - 1)))
                t_gate[(g, hc)] = E_t.n
                wg_rd[hpar] = E_t.n
                sdeps = [(E_t, E_t.n), (E_v, sg_rd[hpar])]
                if g == 0 and hc < 2:
                    sdeps.append((E_v, v_kmul.get(NT - 1)))
                op("scalar", sdeps, E_sc,
                   lambda e, hpar=hpar: e.activation(sg[hpar], P_m1.ap()[:], AF.Silu))
                sc_silu[(g, hc)] = E_sc.n
                silu_last = E_sc.n
                depsup = [(E_v, hmul_last)]
                if g == 0 and hc == 0:
                    depsup.append((E_sc, sc_evacQ.get(NT - 1)))
                for dc in range(NDC):
                    op("tensor", depsup if dc == 0 else [],
                       E_t if dc == NDC - 1 else None,
                       lambda e, dc=dc, hpar=hpar, gpar=gpar: e.matmul(
                           P_m2.ap()[:],
                           wu_sb[hpar][:, dc * 128:(dc + 1) * 128],
                           xnT_g[gpar][:, dc * TG:(dc + 1) * TG],
                           start=(dc == 0), stop=(dc == NDC - 1)))
                t_up[(g, hc)] = E_t.n
                wu_rd[hpar] = E_t.n
                xg_rd[gpar] = E_t.n
                depsh = [(E_t, E_t.n), (E_sc, sc_silu[(g, hc)]), (E_t, down_last)]
                if g == 0 and hc == 0:
                    depsh.append((E_t, t_v.get(NT - 1)))
                op("vector", depsh, E_v,
                   lambda e, hc=hc, hpar=hpar: e.tensor_tensor(
                       bigW[:, hc * TG:(hc + 1) * TG], P_m2.ap()[:], sg[hpar], ALU.mult))
                v_hmul[(g, hc)] = E_v.n
                sg_rd[hpar] = E_v.n
                hmul_last = E_v.n
            if g == 0:
                v_kvbd = build_kvbd()
            for tt in range(NTT):
                ttg = g * NTT + tt
                tpar = tt % 2
                P_dn = P_kv if tpar == 0 else P_rB
                P_de = P_rB if tpar == 0 else P_kv
                xdeps = [(E_v, v_ep2.get(ttg - 2))]
                if ttg < 2:
                    xdeps.append((E_v, v_xn.get(NT - 1)))
                    xdeps.append((E_sc, sc_sq.get(NT - 1)))
                op("sync", xdeps, E_x3[tpar],
                   lambda e, ttg=ttg, tpar=tpar: e.dma_start(
                       out=xb[tpar], in_=x_in[ttg * 128:(ttg + 1) * 128, :]), k=16)
                s_x3[ttg] = (E_x3[tpar], E_x3[tpar].n)
                if ttg >= 1:
                    op("sync", [(E_v, v_ep3[ttg - 1])], E_out[(ttg - 1) % 2],
                       lambda e, tg=ttg - 1: e.dma_start(
                           out=y_out[tg * 128:(tg + 1) * 128, :], in_=outb[tg % 2][:]), k=16)
                    s_out[ttg - 1] = (E_out[(ttg - 1) % 2], E_out[(ttg - 1) % 2].n)
                qtdeps = [(E_t, t_attn.get(ttg - 2)), (E_qt[ttg % 2], E_qt[ttg % 2].n)]
                if ttg < 2:
                    qtdeps += [cv_stage_guard.get(0), cv_stage_guard.get(1)]
                qtdeps = [d for d in qtdeps if d]
                op("scalar", qtdeps, E_qtl[tpar],
                   lambda e, ttg=ttg, tpar=tpar: e.dma_start(out=qT_t[tpar], in_=qT_scr[ttg]), k=16)
                sd_qt[ttg] = (E_qtl[tpar], E_qtl[tpar].n)
                sfdeps = [sd_qt[ttg]]
                if ttg == 0:
                    sfdeps.append((E_v, v_kmul.get(NT // 2 - 1)))
                op("scalar", sfdeps, E_sc,
                   lambda e, tpar=tpar: e.activation(eab[:, 0:D], qT_t[tpar], AF.Exp))
                op("scalar", [(E_sc, E_sc.n)], E_sc,
                   lambda e, tpar=tpar: e.activation(qT_t[tpar], eab[:, 0:D], AF.Ln, bias=cst[:, 1:2]))
                sc_qsoft[ttg] = E_sc.n
                depsd = [(E_v, v_hmul[(g, NHC - 1)]), (E_v, v_ep3.get(ttg - 2)),
                         (E_v, v_ep0.get(ttg - 1))]
                if ttg == 0:
                    depsd.append((E_v, v_ccevac))
                if ttg == 1:
                    depsd.append((E_v, v_vev.get(NT - 1)))
                first = True
                for hc in range(NHC):
                    for hf in range(2):
                        last = (hc == NHC - 1 and hf == 1)
                        op("tensor", depsd if first else [], E_t if last else None,
                           lambda e, hc=hc, hf=hf, tt=tt, P_dn=P_dn: e.matmul(
                               P_dn.ap()[:, hf * 512:(hf + 1) * 512],
                               bigW[:, hc * TG + tt * 128: hc * TG + (tt + 1) * 128],
                               wd_sb[:, hc * 1024 + hf * 512: hc * 1024 + (hf + 1) * 512],
                               start=(hc == 0), stop=(hc == NHC - 1)))
                        first = False
                t_down[ttg] = E_t.n
                down_last = E_t.n
                depsa = [(E_v, v_kvbd), (E_sc, sc_qsoft[ttg]),
                         (E_v, v_ep1.get(ttg - 1)), (E_v, v_ep3.get(ttg - 1))]
                if ttg == 0:
                    depsa += [(E_v, v_vev.get(NT - 1)), (E_v, v_kcopy.get(NT - 1))]
                first = True
                for p in range(8):
                    op("tensor", depsa if first else [], None,
                       lambda e, p=p, tpar=tpar: e.matmul(
                           P_rA.ap()[:, p * 128:(p + 1) * 128],
                           qT_t[tpar][:, p * 128:(p + 1) * 128],
                           kv_bd[:, p * 130: p * 130 + 128],
                           start=(p % 4 == 0), stop=(p % 4 == 3)))
                    first = False
                    op("tensor", [], E_t if p == 7 else None,
                       lambda e, p=p, tpar=tpar, P_de=P_de: e.matmul(
                           P_de.ap()[:, 2 * p: 2 * p + 2],
                           qT_t[tpar][:, p * 128:(p + 1) * 128],
                           kv_bd[:, p * 130 + 128: p * 130 + 130],
                           start=(p == 0), stop=(p == 7)))
                t_attn[ttg] = E_t.n
                epdeps = [(E_t, E_t.n), (E_v, v_ep1.get(ttg - 2))]
                if ttg == 0:
                    epdeps += [(E_v, v_qhat.get(NT - 1)), (E_v, v_kmul.get(NT - 1)),
                               (E_sc, sc_sq.get(NT - 1)), (E_sc, sc_wq), (E_v, v_wq)]
                op("vector", epdeps, E_v,
                   lambda e, tpar=tpar, P_de=P_de: e.reciprocal(rden[tpar][:], P_de.ap()[:, 0:16]))
                v_ep0[ttg] = E_v.n
                op("vector", [(E_v, E_v.n)], E_v,
                   lambda e, tpar=tpar: e.tensor_tensor(
                       ep_scr1.rearrange("p (a b c) -> p a b c", a=8, b=2),
                       P_rA.ap().rearrange("p (a b c) -> p a b c", a=8, b=2),
                       rden[tpar][:].rearrange("p (a b) -> p a b", a=8)
                       .unsqueeze(3).broadcast_to([128, 8, 2, 64]),
                       ALU.mult))
                v_ep1[ttg] = E_v.n
                ep2_deps = [(E_v, E_v.n), s_x3[ttg]]
                if ttg == 0:
                    ep2_deps.append(cv_out_guard.get(7))
                op("vector", [d for d in ep2_deps if d], E_v,
                   lambda e, tpar=tpar: e.tensor_add(ep_scr2, ep_scr1, xb[tpar]))
                v_ep2[ttg] = E_v.n
                ep3deps = [(E_v, E_v.n), (E_t, t_down[ttg])] + (
                    [s_out[ttg - 2]] if ttg >= 2 else [])
                if ttg == 1:
                    ep3deps.append((E_v, v_kvbd))
                op("vector", ep3deps, E_v,
                   lambda e, tpar=tpar, P_dn=P_dn: e.tensor_add(outb[tpar][:], ep_scr2, P_dn.ap()[:]))
                v_ep3[ttg] = E_v.n
        op("sync", [(E_v, v_ep3[NT - 1])], E_out[(NT - 1) % 2],
           lambda e, tg=NT - 1: e.dma_start(
               out=y_out[tg * 128:(tg + 1) * 128, :], in_=outb[tg % 2][:]), k=16)
        s_out[NT - 1] = (E_out[(NT - 1) % 2], E_out[(NT - 1) % 2].n)

        if debug_taps:
            dbg_deps = ([(E_qt[0], E_qt[0].n), (E_qt[1], E_qt[1].n),
                         (E_v, E_v.n), (E_sc, E_sc.n)])
            for t in range(NT):
                op("gpsimd", dbg_deps if t == 0 else [], E_gd,
                   lambda e, t=t: e.dma_start(out=dbg_qt[t], in_=qT_scr[t]), k=16)
            op("gpsimd", [], E_gd, lambda e: e.dma_start(out=dbg_wd, in_=wd_sb[:]), k=16)
            op("gpsimd", [], E_gd, lambda e: e.dma_start(out=dbg_h, in_=bigW[:, 0:NHC * 512]), k=16)
            op("gpsimd", [], E_gd, lambda e: e.dma_start(out=dbg_kvbd, in_=kv_bd[:]), k=16)
            op("gpsimd", [(E_gd, E_gd.n)], None, lambda e: e.nop())
        op("sync", [(E_out[0], E_out[0].n), (E_out[1], E_out[1].n)], None,
           lambda e: e.nop())

        block = es.enter_context(nc.Block())

        @block.sync
        def _(sync):
            for th in P.q["sync"]:
                th(sync)

        @block.vector
        def _(vector):
            for th in P.q["vector"]:
                th(vector)

        @block.scalar
        def _(scalar):
            for th in P.q["scalar"]:
                th(scalar)

        @block.tensor
        def _(tensor):
            for th in P.q["tensor"]:
                th(tensor)

        @block.gpsimd
        def _(gpsimd):
            for th in P.q["gpsimd"]:
                th(gpsimd)

    return nc


def _prep_inputs(L0_attn_x, L0_attn_W_qkv, L0_ffn_w_gate, L0_ffn_w_up, L0_ffn_w_down):
    x = np.asarray(L0_attn_x, dtype=np.float32)
    wq = np.ascontiguousarray(np.asarray(L0_attn_W_qkv, dtype=np.float32).T)
    wg = np.ascontiguousarray(
        np.asarray(L0_ffn_w_gate, dtype=np.int32).T.reshape(NDC, 128, NHC, 128)
        .transpose(2, 1, 0, 3).reshape(NHC, 128, D))
    wu = np.ascontiguousarray(
        np.asarray(L0_ffn_w_up, dtype=np.int32).T.reshape(NDC, 128, NHC, 128)
        .transpose(2, 1, 0, 3).reshape(NHC, 128, D))
    wd = np.ascontiguousarray(np.asarray(L0_ffn_w_down, dtype=np.int32).T)
    ident = np.eye(128, dtype=ml_dtypes.bfloat16)
    b, s, d = x.shape
    half = s // 2
    shards = []
    for c in range(8):
        sh = np.ascontiguousarray(x[c // 2, (c % 2) * half:(c % 2 + 1) * half, :])
        shards.append({"x": sh, "wqkv": wq, "wg": wg, "wu": wu, "wd": wd, "ident": ident})
    return shards, (b, s, d)


def kernel(L0_attn_x, L0_attn_W_qkv, L0_ffn_w_gate, L0_ffn_w_up, L0_ffn_w_down):
    in_maps, (b, s, d) = _prep_inputs(L0_attn_x, L0_attn_W_qkv, L0_ffn_w_gate,
                                      L0_ffn_w_up, L0_ffn_w_down)
    nc = build_program(TCORE=s // 2)
    res = run_bass_kernel_spmd(nc, in_maps, core_ids=list(range(8)))
    out = np.empty((b, s, d), dtype=np.float32)
    half = s // 2
    for c in range(8):
        out[c // 2, (c % 2) * half:(c % 2 + 1) * half, :] = res.results[c]["y"]
    return out


# version 26 (final)
# speedup vs baseline: 1.0753x; 1.0258x over previous
import contextlib
import numpy as np
import ml_dtypes
import concourse.bass as bass
import concourse.mybir as mybir
from concourse.bass_utils import run_bass_kernel_spmd

F32 = mybir.dt.float32
BF16 = mybir.dt.bfloat16
I32 = mybir.dt.int32
AF = mybir.ActivationFunctionType
ALU = mybir.AluOpType
AX = mybir.AxisListType

D = 1024
H = 16
HD = 64
FF = 3072
NHC = 24
NDC = 8
EPS_LN = 1e-5
EPS_ATT = 1e-6
EPS_L2 = 1e-12
SCALE_ATT = float(64 ** -0.5)
DEQ = 1.0 / 127.0

ENGS = ("sync", "vector", "scalar", "tensor", "gpsimd")


class Edge:
    def __init__(self, es, nc, name):
        self.h = es.enter_context(nc.semaphore(name))
        self.n = 0


class Plan:
    def __init__(self):
        self.q = {e: [] for e in ENGS}
        self.lastw = {e: {} for e in ENGS}

    def op(self, eng, deps, inc, build, k=1):
        dd = []
        for ed, v in deps:
            if ed is None or v is None or v <= 0:
                continue
            key = id(ed)
            if self.lastw[eng].get(key, 0) >= v:
                continue
            self.lastw[eng][key] = v
            dd.append((ed.h, v))
        icap = None
        if inc is not None:
            inc.n += k
            icap = (inc.h, k)

        def thunk(eo, dd=tuple(dd), icap=icap, build=build):
            for h, v in dd:
                eo.wait_ge(h, v)
            ins = build(eo)
            if icap is not None:
                ins.then_inc(icap[0], icap[1])

        self.q[eng].append(thunk)
        return inc.n if inc is not None else None


def build_program(TCORE=4096, detect_races=False, debug_taps=False):
    NT = TCORE // 128
    TG = 512
    NTG = TCORE // TG
    NTT = TG // 128

    nc = bass.Bass("TRN2", target_bir_lowering=False, debug=False,
                   detect_race_conditions=detect_races)

    x_in = nc.dram_tensor("x", [TCORE, D], F32, kind="ExternalInput").ap()
    wqkv = nc.dram_tensor("wqkv", [D, 3 * D], F32, kind="ExternalInput").ap()
    wg_i = nc.dram_tensor("wg", [NHC, 128, D], I32, kind="ExternalInput").ap()
    wu_i = nc.dram_tensor("wu", [NHC, 128, D], I32, kind="ExternalInput").ap()
    wd_i = nc.dram_tensor("wd", [FF, D], I32, kind="ExternalInput").ap()
    ident_in = nc.dram_tensor("ident", [128, 128], BF16, kind="ExternalInput").ap()
    y_out = nc.dram_tensor("y", [TCORE, D], F32, kind="ExternalOutput").ap()

    wg_bf = nc.dram_tensor("wg_bf", [NHC, 128, 1024], BF16).ap()
    wu_bf = nc.dram_tensor("wu_bf", [NHC, 128, 1024], BF16).ap()
    qT_scr = nc.dram_tensor("qT_scr", [NT, 128, 1024], BF16).ap()
    xnT_scr = nc.dram_tensor("xnT_scr", [NT, 128, 1024], BF16).ap()
    cc_in = nc.dram_tensor("cc_in", [128, 520], F32)
    cc_out = nc.dram_tensor("cc_out", [128, 520], F32)
    dbg_qt = dbg_wd = dbg_h = dbg_kvbd = None
    if debug_taps:
        dbg_qt = nc.dram_tensor("dbg_qt", [NT, 128, 1024], BF16, kind="ExternalOutput").ap()
        dbg_wd = nc.dram_tensor("dbg_wd", [128, NHC * 1024], BF16, kind="ExternalOutput").ap()
        dbg_h = nc.dram_tensor("dbg_h", [128, NHC * 512], BF16, kind="ExternalOutput").ap()
        dbg_kvbd = nc.dram_tensor("dbg_kvbd", [128, 8 * 130], BF16, kind="ExternalOutput").ap()

    es = contextlib.ExitStack()
    with es:
        def sb(name, shape, dt):
            return es.enter_context(nc.sbuf_tensor(name, shape, dt))

        xb4 = sb("xb4", [128, 4 * D], F32)
        xb = [xb4.ap()[:, i * D:(i + 1) * D] for i in range(4)]
        qn = sb("qn", [128, D], F32)
        kn = sb("kn", [128, D], F32)
        eab = sb("eab", [128, D], BF16)
        ep2b = sb("ep2b", [128, D], F32)
        outb = [sb("outb0", [128, D], F32), sb("outb1", [128, D], F32)]
        sq_k = sb("sq_k", [128, D], BF16)
        xnb = [sb("xnb0", [128, D], BF16), sb("xnb1", [128, D], BF16)]
        qhat = [sb("qhat0", [128, D], BF16), sb("qhat1", [128, D], BF16)]
        khat = [sb(f"khat{i}", [128, D], BF16) for i in range(4)]
        vtile = [sb(f"vt{i}", [128, H * 65], BF16) for i in range(4)]
        xnT_t = [sb("xnTt0", [128, D], BF16), sb("xnTt1", [128, D], BF16),
                 sb("xnTt2", [128, D], BF16), sb("xnTt3", [128, D], BF16)]
        qh_st = [sb("qhst0", [128, D], BF16), sb("qhst1", [128, D], BF16)]
        wcast = [sb("wc0", [128, 1024], BF16), sb("wc1", [128, 1024], BF16)]
        xnT_g = [sb("xnTg0", [128, NDC * TG], BF16), sb("xnTg1", [128, NDC * TG], BF16)]
        _xbt = xb4.ap()[:, 2 * D:4 * D].bitcast(BF16)
        wg_sb = [_xbt[:, 0:1024], _xbt[:, 1024:2048]]
        wu_sb = [_xbt[:, 2048:3072], _xbt[:, 3072:4096]]
        wd_sb = sb("wd_sb", [128, NHC * 1024], BF16)
        bigW = sb("bigW", [128, NDC * 3 * D], BF16)
        kv_bd = sb("kv_bd", [128, 8 * 130], BF16)
        ident = sb("ident_sb", [128, 128], BF16)
        wi = [sb("wi0", [128, 1024], I32), sb("wi1", [128, 1024], I32)]
        qT_t = [wi[0].ap().bitcast(BF16)[:, 0:1024], wi[1].ap().bitcast(BF16)[:, 0:1024]]
        cst = sb("cst", [128, 3], F32)
        ssum = sb("ssum", [128, 4], F32)
        ssq = sb("ssq", [128, 4], F32)
        mb = sb("mb", [128, 4], F32)
        mm2 = sb("mm2", [128, 4], F32)
        varb = sb("varb", [128, 4], F32)
        stdv = sb("stdv", [128, 4], F32)
        rstd = sb("rstd", [128, 4], F32)
        negmr = sb("negmr", [128, 4], F32)
        ssq_q = sb("ssq_q", [128, H], F32)
        ssq_k = sb("ssq_k", [128, H], F32)
        nrm2_q = sb("nrm2_q", [128, H], F32)
        nrm2_k = sb("nrm2_k", [128, H], F32)
        rst_q = sb("rst_q", [128, H], F32)
        rst_k = sb("rst_k", [128, H], F32)
        rden = [sb("rden0", [128, H], F32), sb("rden1", [128, H], F32)]

        sqdump = eab.ap()[:, 0:D]
        cc_sb = outb[1].ap()[:, 0:520]
        sg = [kn.ap()[:, 0:512], kn.ap()[:, 512:1024]]
        ep_scr1 = qn.ap()[:, 0:D]
        ep_scr2 = ep2b.ap()[:]
        P_rA = es.enter_context(nc.psum_tensor("P_rA", [128, 1024], F32))
        P_rB = es.enter_context(nc.psum_tensor("P_rB", [128, 1024], F32))
        P_kv = es.enter_context(nc.psum_tensor("P_kv", [128, 1024], F32))
        P_m1 = es.enter_context(nc.psum_tensor("P_m1", [128, 512], F32))
        P_m2 = es.enter_context(nc.psum_tensor("P_m2", [128, 512], F32))
        P_m1b = P_m1.ap().bitcast(BF16)
        P_m2b = P_m2.ap().bitcast(BF16)

        E_gd = Edge(es, nc, "E_gd")
        E_t = Edge(es, nc, "E_t")
        E_v = Edge(es, nc, "E_v")
        E_sc = Edge(es, nc, "E_sc")
        E_cc = Edge(es, nc, "E_cc")
        E_x = [Edge(es, nc, f"E_x{i}") for i in range(4)]
        E_cvl = [Edge(es, nc, f"E_cvl{i}") for i in range(8)]
        E_cvst = [Edge(es, nc, f"E_cvst{i}") for i in range(8)]
        E_xnt = [Edge(es, nc, f"E_xnt{i}") for i in range(4)]
        E_xnl = [Edge(es, nc, f"E_xnl{i}") for i in range(4)]
        E_qt = [Edge(es, nc, f"E_qt{i}") for i in range(2)]
        E_xg = [Edge(es, nc, f"E_xg{i}") for i in range(2)]
        E_wg = [Edge(es, nc, f"E_wg{i}") for i in range(3)]
        E_wu = [Edge(es, nc, f"E_wu{i}") for i in range(3)]
        E_x3 = [Edge(es, nc, f"E_x3{i}") for i in range(2)]
        E_out = [Edge(es, nc, f"E_out{i}") for i in range(2)]
        E_wq = [Edge(es, nc, f"E_wq{i}") for i in range(2)]
        E_qtl = [Edge(es, nc, f"E_qtl{i}") for i in range(2)]
        E_ccin = Edge(es, nc, "E_ccin")
        E_ccout = Edge(es, nc, "E_ccout")
        E_misc = Edge(es, nc, "E_misc")

        P = Plan()
        op = P.op

        s_x = {}; sc_sq = {}; v_var = {}; sc_sqrt = {}; v_stats = {}; sc_xn = {}
        t_trx = {}; v_evacX = {}; s_xnt = {}
        t_q = {}; sc_qsq = {}; v_gq = {}; sc_nq = {}; v_qmul = {}; sc_expq = {}; sc_qhat = {}
        t_k = {}; sc_ksq = {}; v_gk = {}; sc_nk = {}; v_kmul = {}; sc_expk = {}; sc_khat = {}
        t_v = {}; v_vev = {}; t_kv = {}; t_trq = {}; v_evacQ = {}; s_qt = {}
        gd_wg = {}; sc_cwg = {}; s_swg = {}
        gd_wu = {}; v_cwu = {}; s_swu = {}
        gd_wd = {}; sc_cwd = {}
        t_gate = {}; sc_silu = {}; t_up = {}; v_hmul = {}
        s_w = {}; s_xg = {}; s_x3 = {}; sd_qt = {}
        t_down = {}; t_attn = {}
        v_ep0 = {}; v_ep1 = {}; v_ep2 = {}; v_ep3 = {}; s_out = {}

        op("sync", [], E_misc, lambda e: e.dma_start(out=ident[:], in_=ident_in), k=16)
        s_ident = (E_misc, E_misc.n)
        for _vt in vtile:
            op("vector", [], E_v, lambda e, _vt=_vt: e.memset(_vt[:], 1.0))
        op("vector", [], E_v, lambda e: e.memset(kv_bd[:], 0.0))
        op("vector", [], E_v, lambda e: e.memset(cst[:, 0:1], EPS_LN))
        op("vector", [], E_v, lambda e: e.memset(cst[:, 1:2], 1.0))
        op("vector", [], E_v, lambda e: e.memset(cst[:, 2:3], EPS_L2 * EPS_L2))
        v_init = E_v.n

        wq_cast = {}
        wq_sc_last = [0]
        wq_v_last = [0]

        wq_ld = {}

        def wq_load(kk):
            if kk in wq_ld:
                return
            if kk >= 2 and (kk - 2) not in wq_cast:
                return
            dc, s = divmod(kk, 3)
            par = kk % 2
            deps = []
            if kk >= 2:
                pe, pv = wq_cast[kk - 2]
                deps = [((E_sc if pe == "scalar" else E_v), pv)]
            op("sync", deps, E_wq[par],
               lambda e, dc=dc, s=s, par=par: e.dma_start(
                   out=outb[par][:], in_=wqkv[dc * 128:(dc + 1) * 128, s * 1024:(s + 1) * 1024]), k=16)
            wq_ld[kk] = (E_wq[par], E_wq[par].n)

        def wq_cast_task(kk):
            wq_load(kk)
            par = kk % 2
            dc, s = divmod(kk, 3)
            eng = "scalar" if (kk % 2 == 0) else "vector"
            ed = E_sc if eng == "scalar" else E_v
            col = dc * 3 * D + s * 1024
            if eng == "scalar":
                op(eng, [wq_ld[kk]], ed,
                   lambda e, par=par, col=col: e.activation(
                       bigW[:, col:col + 1024], outb[par][:], AF.Copy))
                wq_sc_last[0] = ed.n
            else:
                op(eng, [wq_ld[kk]], ed,
                   lambda e, par=par, col=col: e.tensor_copy(bigW[:, col:col + 1024], outb[par][:]))
                wq_v_last[0] = ed.n
            wq_cast[kk] = (eng, ed.n)

        def conv_sched(t):
            return range(t * NHC // NT, (t + 1) * NHC // NT)

        sc_qcopy = {}
        v_kcopy = {}
        v_xn = {}
        sc_evacQ = {}
        sc_qsoft = {}
        v_qhat = {}
        sc_khat = {}
        v_gq = {}
        v_gk = {}
        NP = NT // 2

        def front_x(t):
            op("sync", [(E_v, v_xn.get(t - 4)), (E_sc, sc_sq.get(t - 4))], E_x[t % 4],
               lambda e, t=t: e.dma_start(out=xb[t % 4], in_=x_in[t * 128:(t + 1) * 128, :]), k=16)
            s_x[t] = (E_x[t % 4], E_x[t % 4].n)

        def round_mm(t, base, Pr, deps):
            tpar = t % 4
            for dc in range(NDC):
                for hf in range(2):
                    ed = E_t if (dc == NDC - 1 and hf == 1) else None
                    op("tensor", deps if (dc == 0 and hf == 0) else [], ed,
                       lambda e, dc=dc, hf=hf, tpar=tpar, Pr=Pr, base=base: e.matmul(
                           Pr.ap()[:, hf * 512:(hf + 1) * 512],
                           xnT_t[tpar][:, dc * 128:(dc + 1) * 128],
                           bigW[:, dc * 3 * D + base + hf * 512: dc * 3 * D + base + (hf + 1) * 512],
                           start=(dc == 0), stop=(dc == NDC - 1)))
            return E_t.n

        def kv_col(p):
            return (p // 4) * 512 + (p % 4) * 65

        def backT_kv(t):
            deps = [(E_sc, sc_khat[t]), (E_v, v_vev[t])]
            first = True
            for p in range(8):
                for eh in range(2):
                    h = 2 * p + eh
                    st = (t == 0) and (p % 4 == 0)
                    sp = (t == NT - 1) and (p % 4 == 3)
                    last = (p == 7) and (eh == 1)
                    op("tensor", deps if first else [], E_t if last else None,
                       lambda e, p=p, eh=eh, h=h, st=st, sp=sp, t=t: e.matmul(
                           P_kv.ap()[eh * 64:(eh + 1) * 64, kv_col(p):kv_col(p) + 65],
                           khat[t % 4][:, h * 64:(h + 1) * 64],
                           vtile[t % 4][:, h * 65:(h + 1) * 65],
                           start=st, stop=sp, skip_group_check=True))
                    first = False
            t_kv[t] = E_t.n

        cv_in = [wi[0].ap(), wi[1].ap(),
                 qn.ap().bitcast(I32), kn.ap().bitcast(I32),
                 xnT_g[0].ap().bitcast(I32)[:, 0:1024],
                 xnT_g[0].ap().bitcast(I32)[:, 1024:2048],
                 xnT_g[1].ap().bitcast(I32)[:, 0:1024],
                 xnT_g[1].ap().bitcast(I32)[:, 1024:2048]]
        cv_out = [wcast[0].ap()[:], wcast[1].ap()[:], qhat[0].ap()[:], khat[0].ap()[:],
                  qh_st[0].ap()[:], qh_st[1].ap()[:], sq_k.ap()[:],
                  ep2b.ap().bitcast(BF16)[:, 0:1024]]
        NSTG = 8
        cv_ld = {}
        cv_cast = {}
        cv_st = {}
        cv_tasks = []
        for hc in range(NHC):
            cv_tasks += [(0, hc), (1, hc), (2, hc)]

        def conv_store(j):
            kind, hc = cv_tasks[j]
            if kind == 2:
                return
            b = j % NSTG
            pe, pv = cv_cast[j]
            dstd = wg_bf[hc] if kind == 0 else wu_bf[hc]
            op("scalar", [((E_sc if pe == "scalar" else E_v), pv)], E_cvst[b],
               lambda e, dstd=dstd, b=b: e.dma_start(out=dstd, in_=cv_out[b]), k=16)
            cv_st[j] = (E_cvst[b], E_cvst[b].n)

        def conv_task(k):
            kind, hc = cv_tasks[k]
            b = k % NSTG
            deps = []
            if k >= NSTG:
                pe, pv = cv_cast[k - NSTG]
                deps.append(((E_sc if pe == "scalar" else E_v), pv))
                if cv_st[k - NSTG] is not None:
                    deps.append(cv_st[k - NSTG])
            if kind == 0:
                op("gpsimd", deps, E_cvl[b],
                   lambda e, hc=hc, b=b: e.dma_start(out=cv_in[b], in_=wg_i[hc]), k=16)
            elif kind == 1:
                op("gpsimd", deps, E_cvl[b],
                   lambda e, hc=hc, b=b: e.dma_start(out=cv_in[b], in_=wu_i[hc]), k=16)
            else:
                op("gpsimd", deps, E_cvl[b],
                   lambda e, hc=hc, b=b: e.dma_start(
                       out=cv_in[b], in_=wd_i[hc * 128:(hc + 1) * 128, :]), k=16)
            cv_ld[k] = (E_cvl[b], E_cvl[b].n)
            eng = "scalar" if (k % 2 == 0) else "vector"
            ed = E_sc if eng == "scalar" else E_v
            if kind == 2:
                dst = wd_sb.ap()[:, hc * 1024:(hc + 1) * 1024]
            else:
                dst = cv_out[b]
            if eng == "scalar":
                op(eng, [cv_ld[k]], ed,
                   lambda e, b=b, dst=dst: e.activation(dst, cv_in[b], AF.Copy, scale=DEQ))
            else:
                op(eng, [cv_ld[k]], ed,
                   lambda e, b=b, dst=dst: e.tensor_scalar_mul(dst, cv_in[b], DEQ))
            cv_cast[k] = (eng, ed.n)
            if kind == 2:
                cv_st[k] = None
                sc_cwd[hc] = ed.n if eng == "scalar" else None
                g_conv[hc] = (E_sc if eng == "scalar" else E_v, ed.n)
            else:
                conv_store(k)

        def conv(tp):
            k0 = tp * len(cv_tasks) // NP
            k1 = (tp + 1) * len(cv_tasks) // NP
            for k in range(k0, k1):
                conv_task(k)

        g_conv = {}

        qb = [qn.ap().bitcast(BF16)[:, 0:1024], qn.ap().bitcast(BF16)[:, 1024:2048]]
        kb = [kn.ap().bitcast(BF16)[:, 0:1024], kn.ap().bitcast(BF16)[:, 1024:2048]]

        def ladder(t, which):
            par = t % 2
            Pr = ((P_rA if t % 2 == 0 else P_rB) if which == 0
                  else (P_rB if t % 2 == 0 else P_rA))
            if which == 0:
                qc_deps = [(E_t, t_q[t]), (E_v, v_qhat.get(t - 2))]
                if t == 0:
                    qc_deps.append(cv_stage_guard.get(2))
                op("scalar", [d for d in qc_deps if d], E_sc,
                   lambda e, Pr=Pr, par=par: e.copy(qb[par], Pr.ap()[:]))
                sc_qcopy[t] = E_sc.n
                op("scalar", [(E_sc, E_sc.n), (E_v, v_gq.get(t - 1))], E_sc,
                   lambda e, par=par: e.activation(eab[:], qb[par], AF.Square))
                op("vector", [(E_sc, E_sc.n)], E_v,
                   lambda e: e.reduce_sum(
                       ssq_q[:], eab[:].rearrange("p (h d) -> p h d", h=H), axis=AX.X))
                v_gq[t] = E_v.n
                op("scalar", [(E_v, E_v.n)], E_sc,
                   lambda e: e.activation(nrm2_q[:], ssq_q[:], AF.Ln, bias=cst[:, 2:3]))
                op("scalar", [(E_sc, E_sc.n)], E_sc,
                   lambda e: e.activation(rst_q[:], nrm2_q[:], AF.Exp, scale=-0.5))
                qm_deps = [(E_sc, E_sc.n), (E_t, t_trq.get(t - 2))]
                if t == 0:
                    qm_deps.append(cv_out_guard.get(2))
                op("vector", [d for d in qm_deps if d], E_v,
                   lambda e, par=par: e.tensor_tensor(
                       qhat[par][:].rearrange("p (h d) -> p h d", h=H),
                       qb[par].rearrange("p (h d) -> p h d", h=H),
                       rst_q[:].unsqueeze(2).broadcast_to([128, H, HD]),
                       ALU.mult))
                v_qhat[t] = E_v.n
            else:
                kc_deps = [(E_t, t_k[t]), (E_sc, sc_khat.get(t - 2))]
                if t == 0:
                    kc_deps.append(cv_stage_guard.get(3))
                op("vector", [d for d in kc_deps if d], E_v,
                   lambda e, Pr=Pr, par=par: e.tensor_copy(kb[par], Pr.ap()[:]))
                v_kcopy[t] = E_v.n
                ks_deps = [(E_v, E_v.n), (E_v, v_gk.get(t - 1))]
                if t == 0:
                    ks_deps.append(cv_out_guard.get(6))
                op("scalar", [d for d in ks_deps if d], E_sc,
                   lambda e, par=par: e.activation(sq_k[:], kb[par], AF.Square))
                op("vector", [(E_sc, E_sc.n)], E_v,
                   lambda e: e.reduce_sum(
                       ssq_k[:], sq_k[:].rearrange("p (h d) -> p h d", h=H), axis=AX.X))
                v_gk[t] = E_v.n
                op("scalar", [(E_v, E_v.n)], E_sc,
                   lambda e: e.activation(nrm2_k[:], ssq_k[:], AF.Ln, bias=cst[:, 2:3]))
                op("scalar", [(E_sc, E_sc.n)], E_sc,
                   lambda e: e.activation(rst_k[:], nrm2_k[:], AF.Exp, scale=-0.5))
                op("vector", [(E_sc, E_sc.n)], E_v,
                   lambda e, par=par: e.tensor_tensor(
                       eab[:].rearrange("p (h d) -> p h d", h=H),
                       kb[par].rearrange("p (h d) -> p h d", h=H),
                       rst_k[:].unsqueeze(2).broadcast_to([128, H, HD]),
                       ALU.mult))
                v_kmul[t] = E_v.n
                op("scalar", [(E_v, E_v.n)], E_sc,
                   lambda e: e.activation(sq_k[:], eab[:], AF.Exp))
                lk_deps = [(E_sc, E_sc.n), (E_t, t_kv.get(t - 4))]
                if t == 0:
                    lk_deps.append(cv_out_guard.get(3))
                op("scalar", [d for d in lk_deps if d], E_sc,
                   lambda e, t=t: e.activation(khat[t % 4][:], sq_k[:], AF.Ln, bias=cst[:, 1:2]))
                sc_khat[t] = E_sc.n

        def back_tile(t):
            t_q[t] = round_mm(t, 0, P_rA if t % 2 == 0 else P_rB,
                              [s_xnl[t], (E_v, v_kcopy.get(t - 1)),
                               (E_sc, sc_wq), (E_v, v_wq)])
            ladder(t, 0)
            t_k[t] = round_mm(t, D, P_rB if t % 2 == 0 else P_rA,
                              [(E_v, v_vev.get(t - 1))])
            ladder(t, 1)
            t_v[t] = round_mm(t, 2 * D, P_rA if t % 2 == 0 else P_rB,
                              [(E_sc, sc_qcopy[t])])
            op("vector", [(E_t, t_v[t]), (E_t, t_kv.get(t - 4)), (E_v, v_init)], E_v,
               lambda e, t=t: e.tensor_copy(
                   vtile[t % 4][:].rearrange("p (h c) -> p h c", h=H)[:, :, 0:64],
                   (P_rA if t % 2 == 0 else P_rB).ap().rearrange("p (h d) -> p h d", h=H)))
            v_vev[t] = E_v.n

        def trq_block(t):
            Pm = P_m1b if t % 2 == 0 else P_m2b
            depsq = [(E_v, v_qhat[t]), (E_sc, sc_evacQ.get(t - 2))]
            if t < 2:
                depsq.append((E_v, v_evacX.get(NT - 1)))
            for dc in range(NDC):
                op("tensor", depsq if dc == 0 else [],
                   E_t if dc == NDC - 1 else None,
                   lambda e, dc=dc, t=t, Pm=Pm: e.transpose(
                       Pm[:, dc * 128:(dc + 1) * 128],
                       qhat[t % 2][:, dc * 128:(dc + 1) * 128], ident[:]))
            t_trq[t] = E_t.n
            eq_deps = [(E_t, E_t.n)] + ([s_qt[t - 2]] if t >= 2 else [])
            if t < 2:
                eq_deps.append(cv_out_guard.get(4 + t))
            op("scalar", [d for d in eq_deps if d], E_sc,
               lambda e, t=t, Pm=Pm: e.copy(qh_st[t % 2][:], Pm[:, 0:1024]))
            sc_evacQ[t] = E_sc.n
            op("sync", [(E_sc, E_sc.n)], E_qt[t % 2],
               lambda e, t=t: e.dma_start(out=qT_scr[t], in_=qh_st[t % 2][:]), k=16)
            s_qt[t] = (E_qt[t % 2], E_qt[t % 2].n)

        def front_tile(t, i, fp):
            sl = 2 * (fp % 2)
            op("vector", [(E_sc, sc_sqrt[fp]), (E_v, v_stats[fp]),
                          (E_t, t_trx.get(t - 2))], E_v,
               lambda e, i=i, t=t, sl=sl: e.tensor_scalar(
                   xnb[t % 2][:], xb[t % 4], negmr[:, sl + i:sl + i + 1], rstd[:, sl + i:sl + i + 1],
                   ALU.add, ALU.mult))
            v_xn[t] = E_v.n
            Pm = P_m1b if t % 2 == 0 else P_m2b
            depsx = [(E_v, v_xn[t]), (E_v, v_evacX.get(t - 2)), s_ident]
            for dc in range(NDC):
                op("tensor", depsx if dc == 0 else [],
                   E_t if dc == NDC - 1 else None,
                   lambda e, dc=dc, t=t, Pm=Pm: e.transpose(
                       Pm[:, dc * 128:(dc + 1) * 128],
                       xnb[t % 2][:, dc * 128:(dc + 1) * 128], ident[:]))
            t_trx[t] = E_t.n
            op("vector", [(E_t, E_t.n)] + ([s_xnt[t - 4]] if t >= 4 else []), E_v,
               lambda e, t=t, Pm=Pm: e.tensor_copy(xnT_t[t % 4][:], Pm[:, 0:1024]))
            v_evacX[t] = E_v.n
            op("sync", [(E_v, v_evacX[t])], E_xnt[t % 4],
               lambda e, t=t: e.dma_start(out=xnT_scr[t], in_=xnT_t[t % 4][:]), k=16)
            s_xnt[t] = (E_xnt[t % 4], E_xnt[t % 4].n)

        front_x(0)
        front_x(1)
        wq_n = NDC * 3
        wq_np = min(NP, 12)

        def wq_range(fp):
            if fp >= wq_np:
                return range(0)
            return range(fp * wq_n // wq_np, (fp + 1) * wq_n // wq_np)

        for fp in range(NP):
            if fp + 1 < NP:
                front_x(2 * fp + 2)
                front_x(2 * fp + 3)
            for kk in wq_range(fp):
                wq_load(kk)
            if True:
                a, b = 2 * fp, 2 * fp + 1
                sl = 2 * (fp % 2)
                op("scalar", [s_x[a], (E_v, v_stats.get(fp - 2)), (E_v, v_var.get(fp - 2))], E_sc,
                   lambda e, a=a, sl=sl: e.activation(sqdump, xb[a % 4], AF.Square, accum_out=ssq[:, sl:sl + 1]))
                op("scalar", [s_x[b]], E_sc,
                   lambda e, b=b, sl=sl: e.activation(sqdump, xb[b % 4], AF.Square, accum_out=ssq[:, sl + 1:sl + 2]))
                sc_sq[a] = sc_sq[b] = E_sc.n
                op("vector", [s_x[a], s_x[b], (E_v, v_stats.get(fp - 2))], E_v,
                   lambda e, a=a, sl=sl: e.reduce_sum(
                       ssum[:, sl:sl + 2], xb4.ap()[:, (a % 4) * D:(a % 4 + 2) * D]
                       .rearrange("p (i d) -> p i d", i=2), axis=AX.X))
                op("vector", [(E_v, E_v.n)], E_v,
                   lambda e, sl=sl: e.tensor_scalar_mul(mb[:, sl:sl + 2], ssum[:, sl:sl + 2], 1.0 / D))
                op("vector", [(E_v, E_v.n)], E_v,
                   lambda e, sl=sl: e.tensor_mul(mm2[:, sl:sl + 2], mb[:, sl:sl + 2], mb[:, sl:sl + 2]))
                op("vector", [(E_v, E_v.n), (E_sc, sc_sq[b])], E_v,
                   lambda e, sl=sl: e.tensor_scalar(varb[:, sl:sl + 2], ssq[:, sl:sl + 2], 1.0 / D, EPS_LN, ALU.mult, ALU.add))
                op("vector", [(E_v, E_v.n)], E_v,
                   lambda e, sl=sl: e.tensor_sub(varb[:, sl:sl + 2], varb[:, sl:sl + 2], mm2[:, sl:sl + 2]))
                v_var[fp] = E_v.n
                op("scalar", [(E_v, E_v.n)], E_sc,
                   lambda e, sl=sl: e.activation(stdv[:, sl:sl + 2], varb[:, sl:sl + 2], AF.Ln))
                op("scalar", [(E_sc, E_sc.n)], E_sc,
                   lambda e, sl=sl: e.activation(rstd[:, sl:sl + 2], stdv[:, sl:sl + 2], AF.Exp, scale=-0.5))
                sc_sqrt[fp] = E_sc.n
                op("vector", [(E_v, v_var[fp])], E_v,
                   lambda e, sl=sl: e.tensor_scalar_mul(negmr[:, sl:sl + 2], mb[:, sl:sl + 2], -1.0))
                v_stats[fp] = E_v.n
                front_tile(a, 0, fp)
                front_tile(b, 1, fp)
            for kk in wq_range(fp):
                wq_cast_task(kk)
            conv(fp)
        sc_wq = wq_sc_last[0]
        v_wq = wq_v_last[0]

        cv_stage_guard = {}
        cv_out_guard = {}
        for k in range(len(cv_tasks)):
            b = k % NSTG
            pe, pv = cv_cast[k]
            cv_stage_guard[b] = ((E_sc if pe == "scalar" else E_v), pv)
            if cv_st.get(k) is not None:
                cv_out_guard[b] = cv_st[k]
        wd_guards = []
        _smax = _vmax = 0
        for k in range(len(cv_tasks)):
            if cv_tasks[k][0] == 2:
                pe, pv = cv_cast[k]
                if pe == "scalar":
                    _smax = max(_smax, pv)
                else:
                    _vmax = max(_vmax, pv)
        if _smax:
            wd_guards.append((E_sc, _smax))
        if _vmax:
            wd_guards.append((E_v, _vmax))

        s_xnl = {}

        def xnl(t):
            op("sync", [(E_t, t_v.get(t - 4)), (E_xnt[t % 4], E_xnt[t % 4].n)], E_xnl[t % 4],
               lambda e, t=t: e.dma_start(out=xnT_t[t % 4][:], in_=xnT_scr[t]), k=16)
            s_xnl[t] = (E_xnl[t % 4], E_xnl[t % 4].n)

        for ip in range(NP + 2):
            lp = ip if ip < NP else None
            bp = ip - 1 if 1 <= ip <= NP else None
            kp = ip - 2 if ip >= 2 else None
            if lp is not None:
                xnl(2 * lp)
                xnl(2 * lp + 1)
            if kp is not None:
                trq_block(2 * kp)
                trq_block(2 * kp + 1)
            if bp is not None:
                back_tile(2 * bp)
                back_tile(2 * bp + 1)
            if kp is not None:
                backT_kv(2 * kp)
                backT_kv(2 * kp + 1)
        op("vector", [(E_t, t_kv[NT - 1])], E_v,
           lambda e: e.tensor_copy(
               cc_sb.rearrange("p (b q c) -> p b q c", b=2, q=4),
               P_kv.ap().rearrange("p (b f) -> p b f", b=2)[:, :, 0:260]
               .rearrange("p b (q c) -> p b q c", q=4)))
        v_ccevac = E_v.n
        op("scalar", [(E_v, E_v.n)], E_ccin,
           lambda e: e.dma_start(out=cc_in.ap(), in_=cc_sb), k=16)
        op("gpsimd", [(E_ccin, 16)], E_cc,
           lambda e: e.collective_compute(
               "AllReduce", ALU.add,
               replica_groups=[[0, 1], [2, 3], [4, 5], [6, 7]],
               ins=[cc_in.ap().opt()], outs=[cc_out.ap().opt()]))

        ccv = cc_sb.rearrange("p (b q c) -> p b q c", b=2, q=4)

        def build_kvbd():
            deps = [(E_ccout, 16), (E_v, v_init)]
            first = True
            for p in range(8):
                for eh in range(2):
                    op("vector", deps if first else [], E_v,
                       lambda e, p=p, eh=eh: e.tensor_scalar_mul(
                           kv_bd[eh * 64:(eh + 1) * 64, p * 130 + eh * 64: p * 130 + eh * 64 + 64],
                           ccv[eh * 64:(eh + 1) * 64, p // 4, p % 4, 0:64], SCALE_ATT))
                    first = False
                    op("vector", [], E_v,
                       lambda e, p=p, eh=eh: e.tensor_scalar_add(
                           kv_bd[eh * 64:(eh + 1) * 64, p * 130 + 128 + eh: p * 130 + 129 + eh],
                           ccv[eh * 64:(eh + 1) * 64, p // 4, p % 4, 64:65], EPS_ATT))
            return E_v.n

        v_kvbd = None
        wg_rd = [None, None, None]
        wu_rd = [None, None, None]
        wg_slots = [wg_sb[0], wg_sb[1], wcast[0].ap()[:]]
        wu_slots = [wu_sb[0], wu_sb[1], wcast[1].ap()[:]]
        xg_rd = [None, None]
        sg_rd = [None, None]
        silu_last = None
        hmul_last = None
        down_last = None
        def emit_xg(g):
            gpar = g % 2
            depsg = [(ex, ex.n) for ex in E_xnt] + [(E_t, xg_rd[gpar])]
            if g < 2:
                depsg += [d for d in (cv_stage_guard.get(4 + 2 * g),
                                      cv_stage_guard.get(5 + 2 * g)) if d]
            first = True
            for tb in range(NTT):
                op("sync", depsg if first else [], E_xg[gpar],
                   lambda e, tb=tb, g=g, gpar=gpar: e.dma_start(
                       out=xnT_g[gpar][:].rearrange("p (dc tb c) -> p dc tb c", dc=NDC, tb=NTT)
                       [:, :, tb, :],
                       in_=xnT_scr[g * NTT + tb].rearrange("p (dc c) -> p dc c", dc=NDC)), k=16)
                first = False
            s_xg[g] = (E_xg[gpar], E_xg[gpar].n)

        for g in range(NTG):
            gpar = g % 2
            if g == 0:
                emit_xg(0)
            for hc in range(NHC):
                if hc == 12 and g + 1 < NTG:
                    emit_xg(g + 1)
                hpar = hc % 2
                wsl = hc % 3
                if g == 0 and hc == 2:
                    op("scalar", [(E_cc, 1)], E_ccout,
                       lambda e: e.dma_start(out=cc_sb, in_=cc_out.ap()), k=16)
                wgl_deps = [(E_t, wg_rd[wsl])]
                wul_deps = [(E_t, wu_rd[wsl])]
                if g == 0 and hc < 3:
                    cvst_tot = [(ec, ec.n) for ec in E_cvst]
                    wgl_deps += cvst_tot
                    wul_deps += cvst_tot
                    if hc < 2:
                        wgl_deps += [(E_v, v_xn.get(NT - 1)), (E_sc, sc_sq.get(NT - 1))]
                op("sync", wgl_deps, E_wg[wsl],
                   lambda e, hc=hc, wsl=wsl: e.dma_start(out=wg_slots[wsl], in_=wg_bf[hc]), k=16)
                op("sync", wul_deps, E_wu[wsl],
                   lambda e, hc=hc, wsl=wsl: e.dma_start(out=wu_slots[wsl], in_=wu_bf[hc]), k=16)
                s_w[(g, hc)] = [(E_wg[wsl], E_wg[wsl].n), (E_wu[wsl], E_wu[wsl].n)]
                depsgate = s_w[(g, hc)] + [s_xg[g],
                            (E_sc, silu_last),
                            (E_sc, sc_evacQ.get(NT - 2))] + wd_guards
                for dc in range(NDC):
                    op("tensor", depsgate if dc == 0 else [],
                       E_t if dc == NDC - 1 else None,
                       lambda e, dc=dc, wsl=wsl, gpar=gpar: e.matmul(
                           P_m1.ap()[:],
                           wg_slots[wsl][:, dc * 128:(dc + 1) * 128],
                           xnT_g[gpar][:, dc * TG:(dc + 1) * TG],
                           start=(dc == 0), stop=(dc == NDC - 1)))
                t_gate[(g, hc)] = E_t.n
                wg_rd[wsl] = E_t.n
                sdeps = [(E_t, E_t.n), (E_v, sg_rd[hpar])]
                if g == 0 and hc < 2:
                    sdeps.append((E_v, v_kmul.get(NT - 1)))
                op("scalar", sdeps, E_sc,
                   lambda e, hpar=hpar: e.activation(sg[hpar], P_m1.ap()[:], AF.Silu))
                sc_silu[(g, hc)] = E_sc.n
                silu_last = E_sc.n
                depsup = [(E_v, hmul_last)]
                if g == 0 and hc == 0:
                    depsup.append((E_sc, sc_evacQ.get(NT - 1)))
                for dc in range(NDC):
                    op("tensor", depsup if dc == 0 else [],
                       E_t if dc == NDC - 1 else None,
                       lambda e, dc=dc, wsl=wsl, gpar=gpar: e.matmul(
                           P_m2.ap()[:],
                           wu_slots[wsl][:, dc * 128:(dc + 1) * 128],
                           xnT_g[gpar][:, dc * TG:(dc + 1) * TG],
                           start=(dc == 0), stop=(dc == NDC - 1)))
                t_up[(g, hc)] = E_t.n
                wu_rd[wsl] = E_t.n
                xg_rd[gpar] = E_t.n
                depsh = [(E_t, E_t.n), (E_sc, sc_silu[(g, hc)]), (E_t, down_last)]
                if g == 0 and hc == 0:
                    depsh.append((E_t, t_v.get(NT - 1)))
                op("vector", depsh, E_v,
                   lambda e, hc=hc, hpar=hpar: e.tensor_tensor(
                       bigW[:, hc * TG:(hc + 1) * TG], P_m2.ap()[:], sg[hpar], ALU.mult))
                v_hmul[(g, hc)] = E_v.n
                sg_rd[hpar] = E_v.n
                hmul_last = E_v.n
            if g == 0:
                v_kvbd = build_kvbd()
            for tt in range(NTT):
                ttg = g * NTT + tt
                tpar = tt % 2
                P_dn = P_kv if tpar == 0 else P_rB
                P_de = P_rB if tpar == 0 else P_kv
                xdeps = [(E_v, v_ep2.get(ttg - 2))]
                if ttg < 2:
                    xdeps.append((E_v, v_xn.get(NT - 1)))
                    xdeps.append((E_sc, sc_sq.get(NT - 1)))
                op("sync", xdeps, E_x3[tpar],
                   lambda e, ttg=ttg, tpar=tpar: e.dma_start(
                       out=xb[tpar], in_=x_in[ttg * 128:(ttg + 1) * 128, :]), k=16)
                s_x3[ttg] = (E_x3[tpar], E_x3[tpar].n)
                if ttg >= 1:
                    op("sync", [(E_v, v_ep3[ttg - 1])], E_out[(ttg - 1) % 2],
                       lambda e, tg=ttg - 1: e.dma_start(
                           out=y_out[tg * 128:(tg + 1) * 128, :], in_=outb[tg % 2][:]), k=16)
                    s_out[ttg - 1] = (E_out[(ttg - 1) % 2], E_out[(ttg - 1) % 2].n)
                qtdeps = [(E_t, t_attn.get(ttg - 2)), (E_qt[ttg % 2], E_qt[ttg % 2].n)]
                if ttg < 2:
                    qtdeps += [cv_stage_guard.get(0), cv_stage_guard.get(1)]
                qtdeps = [d for d in qtdeps if d]
                op("scalar", qtdeps, E_qtl[tpar],
                   lambda e, ttg=ttg, tpar=tpar: e.dma_start(out=qT_t[tpar], in_=qT_scr[ttg]), k=16)
                sd_qt[ttg] = (E_qtl[tpar], E_qtl[tpar].n)
                sfdeps = [sd_qt[ttg]]
                if ttg == 0:
                    sfdeps.append((E_v, v_kmul.get(NT // 2 - 1)))
                op("scalar", sfdeps, E_sc,
                   lambda e, tpar=tpar: e.activation(eab[:, 0:D], qT_t[tpar], AF.Exp))
                op("scalar", [(E_sc, E_sc.n)], E_sc,
                   lambda e, tpar=tpar: e.activation(qT_t[tpar], eab[:, 0:D], AF.Ln, bias=cst[:, 1:2]))
                sc_qsoft[ttg] = E_sc.n
                depsd = [(E_v, v_hmul[(g, NHC - 1)]), (E_v, v_ep3.get(ttg - 2)),
                         (E_v, v_ep0.get(ttg - 1))]
                if ttg == 0:
                    depsd.append((E_v, v_ccevac))
                if ttg == 1:
                    depsd.append((E_v, v_vev.get(NT - 1)))
                first = True
                for hc in range(NHC):
                    for hf in range(2):
                        last = (hc == NHC - 1 and hf == 1)
                        op("tensor", depsd if first else [], E_t if last else None,
                           lambda e, hc=hc, hf=hf, tt=tt, P_dn=P_dn: e.matmul(
                               P_dn.ap()[:, hf * 512:(hf + 1) * 512],
                               bigW[:, hc * TG + tt * 128: hc * TG + (tt + 1) * 128],
                               wd_sb[:, hc * 1024 + hf * 512: hc * 1024 + (hf + 1) * 512],
                               start=(hc == 0), stop=(hc == NHC - 1)))
                        first = False
                t_down[ttg] = E_t.n
                down_last = E_t.n
                depsa = [(E_v, v_kvbd), (E_sc, sc_qsoft[ttg]),
                         (E_v, v_ep1.get(ttg - 1)), (E_v, v_ep3.get(ttg - 1))]
                if ttg == 0:
                    depsa += [(E_v, v_vev.get(NT - 1)), (E_v, v_kcopy.get(NT - 1))]
                first = True
                for p in range(8):
                    op("tensor", depsa if first else [], None,
                       lambda e, p=p, tpar=tpar: e.matmul(
                           P_rA.ap()[:, p * 128:(p + 1) * 128],
                           qT_t[tpar][:, p * 128:(p + 1) * 128],
                           kv_bd[:, p * 130: p * 130 + 128],
                           start=(p % 4 == 0), stop=(p % 4 == 3)))
                    first = False
                    op("tensor", [], E_t if p == 7 else None,
                       lambda e, p=p, tpar=tpar, P_de=P_de: e.matmul(
                           P_de.ap()[:, 2 * p: 2 * p + 2],
                           qT_t[tpar][:, p * 128:(p + 1) * 128],
                           kv_bd[:, p * 130 + 128: p * 130 + 130],
                           start=(p == 0), stop=(p == 7)))
                t_attn[ttg] = E_t.n
                epdeps = [(E_t, E_t.n), (E_v, v_ep1.get(ttg - 2))]
                if ttg == 0:
                    epdeps += [(E_v, v_qhat.get(NT - 1)), (E_v, v_kmul.get(NT - 1)),
                               (E_sc, sc_sq.get(NT - 1)), (E_sc, sc_wq), (E_v, v_wq)]
                op("vector", epdeps, E_v,
                   lambda e, tpar=tpar, P_de=P_de: e.reciprocal(rden[tpar][:], P_de.ap()[:, 0:16]))
                v_ep0[ttg] = E_v.n
                op("vector", [(E_v, E_v.n)], E_v,
                   lambda e, tpar=tpar: e.tensor_tensor(
                       ep_scr1.rearrange("p (a b c) -> p a b c", a=8, b=2),
                       P_rA.ap().rearrange("p (a b c) -> p a b c", a=8, b=2),
                       rden[tpar][:].rearrange("p (a b) -> p a b", a=8)
                       .unsqueeze(3).broadcast_to([128, 8, 2, 64]),
                       ALU.mult))
                v_ep1[ttg] = E_v.n
                ep2_deps = [(E_v, E_v.n), s_x3[ttg]]
                if ttg == 0:
                    ep2_deps.append(cv_out_guard.get(7))
                op("vector", [d for d in ep2_deps if d], E_v,
                   lambda e, tpar=tpar: e.tensor_add(ep_scr2, ep_scr1, xb[tpar]))
                v_ep2[ttg] = E_v.n
                ep3deps = [(E_v, E_v.n), (E_t, t_down[ttg])] + (
                    [s_out[ttg - 2]] if ttg >= 2 else [])
                if ttg == 1:
                    ep3deps.append((E_v, v_kvbd))
                op("vector", ep3deps, E_v,
                   lambda e, tpar=tpar, P_dn=P_dn: e.tensor_add(outb[tpar][:], ep_scr2, P_dn.ap()[:]))
                v_ep3[ttg] = E_v.n
        op("sync", [(E_v, v_ep3[NT - 1])], E_out[(NT - 1) % 2],
           lambda e, tg=NT - 1: e.dma_start(
               out=y_out[tg * 128:(tg + 1) * 128, :], in_=outb[tg % 2][:]), k=16)
        s_out[NT - 1] = (E_out[(NT - 1) % 2], E_out[(NT - 1) % 2].n)

        if debug_taps:
            dbg_deps = ([(E_qt[0], E_qt[0].n), (E_qt[1], E_qt[1].n),
                         (E_v, E_v.n), (E_sc, E_sc.n)])
            for t in range(NT):
                op("gpsimd", dbg_deps if t == 0 else [], E_gd,
                   lambda e, t=t: e.dma_start(out=dbg_qt[t], in_=qT_scr[t]), k=16)
            op("gpsimd", [], E_gd, lambda e: e.dma_start(out=dbg_wd, in_=wd_sb[:]), k=16)
            op("gpsimd", [], E_gd, lambda e: e.dma_start(out=dbg_h, in_=bigW[:, 0:NHC * 512]), k=16)
            op("gpsimd", [], E_gd, lambda e: e.dma_start(out=dbg_kvbd, in_=kv_bd[:]), k=16)
            op("gpsimd", [(E_gd, E_gd.n)], None, lambda e: e.nop())
        op("sync", [(E_out[0], E_out[0].n), (E_out[1], E_out[1].n)], None,
           lambda e: e.nop())

        block = es.enter_context(nc.Block())

        @block.sync
        def _(sync):
            for th in P.q["sync"]:
                th(sync)

        @block.vector
        def _(vector):
            for th in P.q["vector"]:
                th(vector)

        @block.scalar
        def _(scalar):
            for th in P.q["scalar"]:
                th(scalar)

        @block.tensor
        def _(tensor):
            for th in P.q["tensor"]:
                th(tensor)

        @block.gpsimd
        def _(gpsimd):
            for th in P.q["gpsimd"]:
                th(gpsimd)

    return nc


def _prep_inputs(L0_attn_x, L0_attn_W_qkv, L0_ffn_w_gate, L0_ffn_w_up, L0_ffn_w_down):
    x = np.asarray(L0_attn_x, dtype=np.float32)
    wq = np.ascontiguousarray(np.asarray(L0_attn_W_qkv, dtype=np.float32).T)
    wg = np.ascontiguousarray(
        np.asarray(L0_ffn_w_gate, dtype=np.int32).T.reshape(NDC, 128, NHC, 128)
        .transpose(2, 1, 0, 3).reshape(NHC, 128, D))
    wu = np.ascontiguousarray(
        np.asarray(L0_ffn_w_up, dtype=np.int32).T.reshape(NDC, 128, NHC, 128)
        .transpose(2, 1, 0, 3).reshape(NHC, 128, D))
    wd = np.ascontiguousarray(np.asarray(L0_ffn_w_down, dtype=np.int32).T)
    ident = np.eye(128, dtype=ml_dtypes.bfloat16)
    b, s, d = x.shape
    half = s // 2
    shards = []
    for c in range(8):
        sh = np.ascontiguousarray(x[c // 2, (c % 2) * half:(c % 2 + 1) * half, :])
        shards.append({"x": sh, "wqkv": wq, "wg": wg, "wu": wu, "wd": wd, "ident": ident})
    return shards, (b, s, d)


def kernel(L0_attn_x, L0_attn_W_qkv, L0_ffn_w_gate, L0_ffn_w_up, L0_ffn_w_down):
    in_maps, (b, s, d) = _prep_inputs(L0_attn_x, L0_attn_W_qkv, L0_ffn_w_gate,
                                      L0_ffn_w_up, L0_ffn_w_down)
    nc = build_program(TCORE=s // 2)
    res = run_bass_kernel_spmd(nc, in_maps, core_ids=list(range(8)))
    out = np.empty((b, s, d), dtype=np.float32)
    half = s // 2
    for c in range(8):
        out[c // 2, (c % 2) * half:(c % 2 + 1) * half, :] = res.results[c]["y"]
    return out
